# version 29
# baseline (speedup 1.0000x reference)
import numpy as np
import ml_dtypes
import concourse.bass as bass
import concourse.mybir as mybir
from concourse.bass_utils import run_bass_kernel_spmd

F32 = mybir.dt.float32
BF16 = mybir.dt.bfloat16
AF = mybir.ActivationFunctionType
ALU = mybir.AluOpType
AX = mybir.AxisListType

N_CORES = 8
SEQ = 16384
D = 1024
TPC = SEQ // N_CORES
NT = TPC // 128
NP = 32
RK = 16
ER = NP * RK
EPS = 1e-5

ENGS = ("sync", "scalar", "vector", "tensor", "gpsimd")

_DEBUG_TAPS = ()
_DEBUG_RWKV = False


class Sched:

    def __init__(self):
        self.steps = {e: [] for e in ENGS}
        from collections import defaultdict
        self.cnt = defaultdict(int)

    def op(self, eng, emit, waits=(), inc=None, inc_by=1):
        waits = tuple((s, v) for (s, v) in waits if v > 0)
        self.steps[eng].append((emit, waits, inc, inc_by))
        if inc is not None:
            self.cnt[inc] += inc_by
            return self.cnt[inc]
        return None

    def run(self, e, eng, sems):
        lastw = {}
        for emit, waits, inc, inc_by in self.steps[eng]:
            for s, v in waits:
                if lastw.get(s, -1) < v:
                    e.wait_ge(sems[s], v)
                    lastw[s] = v
            inst = emit(e)
            if inc is not None:
                assert inst is not None
                inst.then_inc(sems[inc], inc_by)


def build_program():
    nc = bass.Bass("TRN2", target_bir_lowering=False, debug=False)

    x_h = nc.dram_tensor("x", [TPC, D], F32, kind="ExternalInput").ap()
    wk_h = nc.dram_tensor("wk", [D, D], F32, kind="ExternalInput").ap()
    wv_h = nc.dram_tensor("wv", [D, D], F32, kind="ExternalInput").ap()
    wr_h = nc.dram_tensor("wr", [D, D], F32, kind="ExternalInput").ap()
    wo_h = nc.dram_tensor("wo", [D, D], F32, kind="ExternalInput").ap()
    bm_h = nc.dram_tensor("bm", [D, ER], F32, kind="ExternalInput").ap()
    am_h = nc.dram_tensor("am", [ER, D], F32, kind="ExternalInput").ap()
    pm_h = nc.dram_tensor("pm", [D, NP], F32, kind="ExternalInput").ap()
    bias_h = nc.dram_tensor("bias", [NP, D], F32, kind="ExternalInput").ap()
    crow_h = nc.dram_tensor("crow", [1, 2608], F32, kind="ExternalInput").ap()
    bigc_h = nc.dram_tensor("bigc", [128, 512], BF16, kind="ExternalInput").ap()
    ohp_h = nc.dram_tensor("ohp", [16, NT * 128], BF16, kind="ExternalInput").ap()
    lt16_h = nc.dram_tensor("lt16", [16, 16], F32, kind="ExternalInput").ap()
    ones128_h = nc.dram_tensor("ones128", [128, 1], F32, kind="ExternalInput").ap()
    emat_h = nc.dram_tensor("emat", [NP, ER], BF16, kind="ExternalInput").ap()

    y_h = nc.dram_tensor("y", [TPC, D], F32, kind="ExternalOutput").ap()

    dbg_rwkv = (nc.dram_tensor("dbg_rwkv", [NT, 128, D], BF16, kind="ExternalOutput").ap()
                if _DEBUG_RWKV else None)
    dbg_wkv = (nc.dram_tensor("dbg_wkv", [NT, 128, D], F32, kind="ExternalOutput").ap()
               if _DEBUG_RWKV else None)
    dbg_rb = (nc.dram_tensor("dbg_rb", [NT, 128, D], BF16, kind="ExternalOutput").ap()
              if _DEBUG_RWKV else None)
    xnt_hbm = nc.dram_tensor("xnt_hbm", [NT, 128, D], BF16, kind="Internal").ap()
    u_hbm = nc.dram_tensor("u_hbm", [NT, 128, D], BF16, kind="Internal").ap()
    cc_in = nc.dram_tensor("cc_in", [1, D], F32, kind="Internal")
    cc_out = nc.dram_tensor("cc_out", [8, D], F32, kind="Internal")

    from contextlib import ExitStack
    ctx = ExitStack()
    sb = lambda name, shape, dt: ctx.enter_context(nc.sbuf_tensor(name, shape, dt))

    wk_s = sb("wk_s", [128, 8, D], BF16)
    wv_s = sb("wv_s", [128, 8, D], BF16)
    wr_s = sb("wr_s", [128, 8, D], BF16)
    wo_s = sb("wo_s", [128, 8, D], BF16)
    b_s = sb("b_s", [128, 8, ER], BF16)
    a_s = sb("a_s", [128, 4, D], BF16)
    pm_s = sb("pm_s", [128, 8, NP], BF16)
    bias_s = sb("bias_s", [NP, D], BF16)

    big_s = sb("big_s", [128, 512], BF16)
    ohp_s = sb("ohp_s", [16, NT * 128], BF16)
    lt16_s = sb("lt16_s", [16, 16], F32)
    ones128_s = sb("ones128_s", [128, 1], F32)
    crow_s = sb("crow_s", [1, 2608], F32)
    lt128_v = big_s[:, 0:128]
    id128_v = big_s[:, 128:256]
    td_v = crow_s[0:1, 0:1024]
    tf_v = crow_s[0:1, 1024:2048]
    temp_v = crow_s[0:1, 2048:2049]
    qneg_v = crow_s[0:1, 2049:2057]
    iota_v = crow_s[0:1, 2064:2192]
    niota_v = crow_s[0:1, 2192:2320]
    iotam1_v = crow_s[0:1, 2320:2448]
    iota16_v = crow_s[0:1, 2448:2464]
    ones1x16_v = crow_s[0:1, 2464:2480]
    ones1x128_v = crow_s[0:1, 2480:2608]
    ed_s = sb("ed_s", [1, D], F32)
    rowA = sb("rowA", [1, D], F32)
    rowB = sb("rowB", [1, D], F32)
    rowC = sb("rowC", [1, D], F32)
    dneg_s = sb("dneg_s", [128, D], F32)
    dm1_s = sb("dm1_s", [128, D], F32)
    eud_s = sb("eud_s", [128, D], F32)
    r16_s = sb("r16_s", [16, D], F32)
    f2_s = sb("f2_s", [16, D], F32)
    sh_s = sb("sh_s", [16, D], F32)
    cr_s = sb("cr_s", [16, D], BF16)
    cr0_s = sb("cr0_s", [1, D], F32)
    x2all = sb("x2all", [128, NT], F32)
    p2row = sb("p2row", [1, NP], F32)
    p2bc = sb("p2bc", [128, NP], F32)
    tinvb = sb("tinvb", [128, 1], F32)
    rows = sb("rows", [128, 40], F32)
    mrow = sb("mrow", [1, 8], F32)
    wsall = sb("wsall", [128, NT * NP], BF16)
    xa = sb("xa", [128, D], F32)
    xb = sb("xb", [128, D], F32)
    f1 = sb("f1", [128, D], F32)
    f2x = sb("f2x", [128, D], F32)
    f3 = sb("f3", [128, D], F32)
    f4 = sb("f4", [128, D], F32)
    f5 = sb("f5", [128, D], F32)
    f6 = sb("f6", [128, D], F32)
    x_t = [xa, xb]
    sall_s = f1
    gall_s = f5
    prod8_s = f6
    xn0 = sb("xn0", [128, D], BF16)
    xn1 = sb("xn1", [128, D], BF16)
    xnt0 = sb("xnt0", [128, D], BF16)
    xnt1 = sb("xnt1", [128, D], BF16)
    u0 = sb("u0", [128, D], BF16)
    u1 = sb("u1", [128, D], BF16)
    rwkvT0 = sb("rwkvT0", [128, D], BF16)
    rwkvT1 = sb("rwkvT1", [128, D], BF16)
    xn_t = [xn0, xn1]
    xnt_t = [xnt0, xnt1]
    u_t = [u0, u1]
    rwkvT_t = [rwkvT0, rwkvT1]
    rwkv_t = [xn0, xn1]
    h_s = sb("h_s", [128, ER], BF16)
    g_bf = sb("g_bf", [128, ER], BF16)
    gt0 = sb("gt0", [128, ER], BF16)
    gt1 = sb("gt1", [128, ER], BF16)
    gt2 = sb("gt2", [128, ER], BF16)
    gT_t = [gt0, gt1, gt2]
    wt0 = sb("wt0", [NP, 128], BF16)
    wt1 = sb("wt1", [NP, 128], BF16)
    wt2 = sb("wt2", [NP, 128], BF16)
    wselT_t = [wt0, wt1, wt2]
    sm1 = sb("sm1", [128, NP], F32)
    sm2 = sb("sm2", [128, NP], F32)
    sm3 = sb("sm3", [128, NP], F32)
    sm4 = sb("sm4", [128, NP], F32)
    sm5 = sb("sm5", [128, NP], F32)
    smA = sb("smA", [128, NT * NP], F32)
    rb0 = sb("rb0", [128, D], BF16)
    rb1 = sb("rb1", [128, D], BF16)
    rb2 = sb("rb2", [128, D], BF16)
    rb_t = [rb0, rb1, rb2]


    ps_w = ctx.enter_context(nc.psum_tensor("ps_w", [128, 512], F32))
    ps_a = [ctx.enter_context(nc.psum_tensor(f"ps_a{h}", [128, 512], F32)) for h in range(2)]
    ps_b = [ctx.enter_context(nc.psum_tensor(f"ps_b{h}", [128, 512], F32)) for h in range(2)]
    ps_t = ctx.enter_context(nc.psum_tensor("ps_t", [128, 1024], BF16))
    cs_ctx = ExitStack()
    ps_cs = [cs_ctx.enter_context(nc.psum_tensor(f"ps_cs{h}", [16, 512], F32)) for h in range(2)]

    sem_ctx = ExitStack()
    sems = {s: sem_ctx.enter_context(nc.semaphore(f"s_{s}")) for s in
            ("pe", "dv", "ac", "pl", "cc", "chv", "chs", "chp",
             "ct", "swk", "swv", "spm", "wt", "cw", "dbg",
             "sx0", "sx1", "sn0", "sn1", "su0", "su1",
             "xs0", "xs1", "uss0", "uss1", "ys0", "ys1")}

    S = Sched()
    H = lambda h: slice(h * 512, (h + 1) * 512)

    ccnt = {"chv": 0, "chs": 0, "chp": 0}

    def CH(e, inst, which="chv"):
        ccnt[which] += 1
        inst.then_inc(sems[which], 1)
        e.wait_ge(sems[which], ccnt[which])
        return inst

    S.op("sync", lambda e: e.dma_start(out=crow_s[:, :], in_=crow_h), inc="ct", inc_by=16)
    S.op("sync", lambda e: e.dma_start(out=ones128_s[:, :], in_=ones128_h), inc="ct", inc_by=16)
    m_cearly = S.cnt["ct"]
    S.op("sync", lambda e: e.dma_start(out=big_s[:, :], in_=bigc_h), inc="ct", inc_by=16)
    S.op("sync", lambda e: e.dma_start(out=ohp_s[:, :], in_=ohp_h), inc="ct", inc_by=16)
    S.op("sync", lambda e: e.dma_start(out=lt16_s[:, :], in_=lt16_h), inc="ct", inc_by=16)
    m_consts = S.cnt["ct"]

    def wload(dst, src, p=128):
        return lambda e: e.dma_start(out=dst[:, :, :], in_=src.rearrange("(c p) e -> p c e", p=p))
    S.op("gpsimd", wload(pm_s, pm_h), inc="spm", inc_by=16)
    S.op("gpsimd", wload(wk_s, wk_h), inc="swk", inc_by=16)
    S.op("gpsimd", wload(wv_s, wv_h), inc="swv", inc_by=16)
    S.op("gpsimd", wload(wr_s, wr_h), inc="wt", inc_by=16)
    S.op("gpsimd", wload(wo_s, wo_h), inc="wt", inc_by=16)
    S.op("gpsimd", wload(b_s, bm_h), inc="wt", inc_by=16)
    S.op("gpsimd", wload(a_s, am_h), inc="wt", inc_by=16)
    S.op("gpsimd", lambda e: e.dma_start(out=bias_s[:, :], in_=bias_h), inc="wt", inc_by=16)
    m_wall = S.cnt["wt"]

    S.op("scalar", lambda e: e.activation(ed_s[:, :], td_v, AF.Exp),
         waits=[("ct", m_cearly)], inc="ac")
    m_ed = S.cnt["ac"]
    S.op("scalar", lambda e: e.activation(rowA[:, :], ed_s[:, :], AF.Exp, scale=-2047.0),
         waits=[("ac", m_ed)], inc="ac")
    m_d2047 = S.cnt["ac"]
    S.op("scalar", lambda e: e.activation(rowB[:, :], ed_s[:, :], AF.Exp, scale=-1.0), inc="ac")
    m_d1 = S.cnt["ac"]

    def outer2(ps, lhs_ap, rhs_row):
        def f(e):
            r = None
            for h in range(2):
                r = e.matmul(ps[h][:, :], lhs_ap, rhs_row[0:1, H(h)], start=True, stop=True)
            return r
        return f

    def exp_evac(dst, ps, scale=1.0, rows_=128):
        def f(e):
            r = None
            for h in range(2):
                r = e.activation(dst[0:rows_, H(h)], ps[h][0:rows_, :], AF.Exp, scale=scale)
            return r
        return f

    S.op("tensor", outer2(ps_a, iota_v, ed_s), waits=[("ac", m_ed), ("ct", m_cearly)], inc="pe")
    m_o1 = S.cnt["pe"]
    S.op("scalar", exp_evac(dneg_s, ps_a), waits=[("pe", m_o1)], inc="ac")
    m_dneg = S.cnt["ac"]
    S.op("tensor", outer2(ps_b, iotam1_v, ed_s), inc="pe")
    m_o2 = S.cnt["pe"]
    S.op("scalar", exp_evac(dm1_s, ps_b, scale=-1.0), waits=[("pe", m_o2)], inc="ac")
    m_dm1 = S.cnt["ac"]

    def eud_outer(e):
        r = None
        for h in range(2):
            e.matmul(ps_a[h][:, :], niota_v, ed_s[0:1, H(h)], start=True, stop=False)
            r = e.matmul(ps_a[h][:, :], ones1x128_v, crow_s[0:1, 1024 + 512 * h:1024 + 512 * (h + 1)], start=False, stop=True)
        return r
    S.op("tensor", eud_outer, waits=[("ac", m_dneg)], inc="pe")
    m_o3 = S.cnt["pe"]
    S.op("scalar", exp_evac(eud_s, ps_a), waits=[("pe", m_o3)], inc="ac")
    m_eud = S.cnt["ac"]

    def r16_outer(e):
        r = None
        for h in range(2):
            r = e.matmul(ps_cs[h][:, :], iota16_v, ed_s[0:1, H(h)], start=True, stop=True)
        return r
    S.op("tensor", r16_outer, inc="pe")
    m_o4 = S.cnt["pe"]

    def r16f2_evac(e):
        r = None
        for h in range(2):
            e.activation(r16_s[0:16, H(h)], ps_cs[h][0:16, :], AF.Exp)
            r = e.activation(f2_s[0:16, H(h)], ps_cs[h][0:16, :], AF.Exp, scale=-1.0)
        return r
    S.op("scalar", r16f2_evac, waits=[("pe", m_o4)], inc="ac")
    m_r16f2 = S.cnt["ac"]

    S.op("scalar", lambda e: e.activation(f4[:, 0:8 * NP], pm_s[:, :, :].rearrange("p c e -> p (c e)"), AF.Square),
         waits=[("spm", 16)], inc="ac")
    m_sqp = S.cnt["ac"]

    def p2_colsum(e):
        r = None
        for c in range(8):
            r = e.matmul(ps_w[0:1, 0:NP], ones128_s[:, :], f4[:, c * NP:(c + 1) * NP],
                         start=(c == 0), stop=(c == 7))
        return r
    S.op("tensor", p2_colsum, waits=[("ac", m_sqp)], inc="pe")
    m_p2 = S.cnt["pe"]
    S.op("vector", lambda e: e.tensor_copy(p2row[:, :], ps_w[0:1, 0:NP]), waits=[("pe", m_p2)], inc="dv")
    m_p2row = S.cnt["dv"]
    S.op("tensor", lambda e: e.matmul(ps_w[:, 0:NP], ones1x128_v, p2row[:, :], start=True, stop=True),
         waits=[("dv", m_p2row)], inc="pe")
    m_p2b = S.cnt["pe"]
    S.op("vector", lambda e: e.tensor_copy(p2bc[:, :], ps_w[:, 0:NP]), waits=[("pe", m_p2b)], inc="dv")
    m_p2bc = S.cnt["dv"]

    S.op("scalar", lambda e: e.activation(mrow[0:1, 0:1], temp_v, AF.Abs), inc="ac")
    m_t1 = S.cnt["ac"]

    def temp_chain(e):
        CH(e, e.tensor_scalar_max(mrow[0:1, 1:2], mrow[0:1, 0:1], 0.1))
        return e.reciprocal(mrow[0:1, 2:3], mrow[0:1, 1:2])
    S.op("vector", temp_chain, waits=[("ac", m_t1)], inc="dv")
    m_t3 = S.cnt["dv"]
    S.op("tensor", lambda e: e.matmul(ps_w[:, 0:1], ones1x128_v, mrow[0:1, 2:3], start=True, stop=True),
         waits=[("dv", max(m_t3, m_p2bc))], inc="pe")
    m_tb = S.cnt["pe"]
    S.op("vector", lambda e: e.tensor_copy(tinvb[:, :], ps_w[:, 0:1]), waits=[("pe", m_tb)], inc="dv")
    m_tinv = S.cnt["dv"]

    def proj(ps, w_s, xnt):
        def f(e):
            r = None
            for h in range(2):
                for c in range(8):
                    r = e.matmul(ps[h][:, :], xnt[:, c * 128:(c + 1) * 128],
                                 w_s[:, c, H(h)], start=(c == 0), stop=(c == 7))
            return r
        return f

    mk = dict(xA=[0] * NT, sq=[0] * NT, var=[0] * NT, sd=[0] * NT,
              xn=[0] * NT, xT=[0] * NT, xevac=[0] * NT, xnst=[0] * NT,
              kmm=[0] * NT, vmm=[0] * NT, kevac=[0] * NT, kv=[0] * NT,
              u=[0] * NT, ust=[0] * NT, cs=[0] * NT, sco=[0] * NT,
              d2p=[0] * NT, rB=[0] * NT, rC=[0] * NT, rsq=[0] * NT,
              rdd=[0] * NT, rsg=[0] * NT, rw1=[0] * NT, wsel=[0] * NT,
              wT=[0] * NT, wTev=[0] * NT)

    def a_load(t):
        if not (0 <= t < NT):
            return
        xw = []
        if t >= 2:
            xw += [("dv", mk["xn"][t - 2]), ("ac", mk["sq"][t - 2])]
        mk["xA"][t] = S.op("sync", (lambda e, t=t: e.dma_start(
            out=x_t[t % 2][:, :], in_=x_h[t * 128:(t + 1) * 128, :])),
            waits=xw, inc="sx%d" % (t % 2), inc_by=16)

    def a_ln(t):
        if not (0 <= t < NT):
            return
        B = 12 * (t % 2)
        R = lambda i: rows[:, B + i:B + i + 1]
        sqw = [("sx%d" % (t % 2), mk["xA"][t])]
        if t == 0:
            sqw.append(("pe", m_p2))
        mk["sq"][t] = S.op("scalar", (lambda e, t=t: e.activation(
            f4[:, :], x_t[t % 2][:, :], AF.Square, accum_out=R(1))),
            waits=sqw, inc="ac")

        def rows_chain(e, t=t, R=R):
            CH(e, e.reduce_sum(R(0), x_t[t % 2][:, :], AX.X))
            CH(e, e.scalar_tensor_tensor(R(4), R(0), 1.0 / (D * D), R(0), ALU.mult, ALU.mult))
            CH(e, e.scalar_tensor_tensor(R(5), R(1), 1.0 / D, R(4), ALU.mult, ALU.subtract))
            return e.tensor_scalar_add(R(6), R(5), EPS)
        lnw = [("ac", mk["sq"][t]), ("sx%d" % (t % 2), mk["xA"][t])]
        mk["var"][t] = S.op("vector", rows_chain, waits=lnw, inc="dv")

        def lnexp(e, R=R):
            CH(e, e.activation(R(9), R(6), AF.Ln), "chs")
            return e.activation(R(10), R(9), AF.Exp, scale=-0.5)
        m_rst = S.op("scalar", lnexp, waits=[("dv", mk["var"][t])], inc="ac")
        mk["sd"][t] = m_rst

        def rows_chain2(e, t=t, R=R):
            CH(e, e.scalar_tensor_tensor(R(11), R(0), 1.0 / D, R(10), ALU.mult, ALU.mult))
            CH(e, e.tensor_mul(R(7), R(10), R(10)))
            CH(e, e.scalar_tensor_tensor(x2all[:, t:t + 1], R(5), float(D), R(7), ALU.mult, ALU.mult))
            return e.tensor_scalar(xn_t[t % 2][:, :], x_t[t % 2][:, :],
                                   R(10), R(11), ALU.mult, ALU.subtract)
        xw = [("ac", m_rst)]
        if t >= 2:
            xw.append(("pe", mk["xT"][t - 2]))
        mk["xn"][t] = S.op("vector", rows_chain2, waits=xw, inc="dv")
        mk["nmr"] = mk.get("nmr", [0] * NT)
        mk["nmr"][t] = mk["xn"][t]

    def a_xt(t):
        if not (0 <= t < NT):
            return
        tw = [("dv", mk["xn"][t])]
        if t >= 1:
            tw.append(("ac", mk["xevac"][t - 1]))
        else:
            tw.append(("ct", m_consts))

        def xtrans(e, t=t):
            r = None
            for c in range(8):
                r = e.transpose(ps_t[:, c * 128:(c + 1) * 128], xn_t[t % 2][:, c * 128:(c + 1) * 128], id128_v)
            return r
        mk["xT"][t] = S.op("tensor", xtrans, waits=tw, inc="pe")

    def a_xevac(t):
        if not (0 <= t < NT):
            return
        ev_w = [("pe", mk["xT"][t])]
        if t >= 2:
            ev_w += [("pe", mk["sco"][t - 2]), ("xs%d" % (t % 2), mk["xnst"][t - 2])]
        mk["xevac"][t] = S.op("scalar", (lambda e, t=t: e.copy(xnt_t[t % 2][:, :], ps_t[:, :])),
                              waits=ev_w, inc="ac")
        mk["xnst"][t] = S.op("scalar", (lambda e, t=t: e.dma_start(
            out=xnt_hbm[t, :, :], in_=xnt_t[t % 2][:, :])), inc="xs%d" % (t % 2), inc_by=16)

    def a_kv(t):
        if not (0 <= t < NT):
            return
        kw = [("ac", mk["xevac"][t]), ("swk", 16)]
        kw.append(("ac", mk["kevac"][t - 1]) if t >= 1 else ("ac", m_eud))
        mk["kmm"][t] = S.op("tensor", proj(ps_a, wk_s, xnt_t[t % 2][:, :]), waits=kw, inc="pe")
        vw = [("swv", 16)]
        vw.append(("dv", mk["kv"][t - 1]) if t >= 1 else ("ac", m_dm1))
        mk["vmm"][t] = S.op("tensor", proj(ps_b, wv_s, xnt_t[t % 2][:, :]), waits=vw, inc="pe")

        def kevac(e):
            r = None
            for h in range(2):
                r = e.copy(f1[:, H(h)], ps_a[h][:, :])
            return r
        kev = [("pe", mk["kmm"][t])]
        if t >= 1:
            kev.append(("dv", mk["kv"][t - 1]))
        mk["kevac"][t] = S.op("scalar", kevac, waits=kev, inc="ac")

        def kvmul(e):
            r = None
            for h in range(2):
                r = e.tensor_mul(f2x[:, H(h)], ps_b[h][:, :], f1[:, H(h)])
            return r
        kvw = [("pe", mk["vmm"][t]), ("ac", mk["kevac"][t])]
        if t >= 1:
            kvw.append(("pl", mk["u"][t - 1]))
        mk["kv"][t] = S.op("vector", kvmul, waits=kvw, inc="dv")

    def a_u(t):
        if not (0 <= t < NT):
            return
        uw = [("dv", mk["kv"][t]), ("ac", m_dneg)]
        if t >= 2:
            uw += [("pe", mk["cs"][t - 2]), ("uss%d" % (t % 2), mk["ust"][t - 2])]
        mk["u"][t] = S.op("gpsimd", (lambda e, t=t: e.tensor_tensor(
            u_t[t % 2][:, :], f2x[:, :], dneg_s[:, :], ALU.mult)), waits=uw, inc="pl")
        mk["ust"][t] = S.op("gpsimd", (lambda e, t=t: e.dma_start(
            out=u_hbm[t, :, :], in_=u_t[t % 2][:, :])), inc="uss%d" % (t % 2), inc_by=16)

    def a_sco(t):
        if not (0 <= t < NT):
            return

        def scomm(e, t=t):
            r = None
            for c in range(8):
                r = e.matmul(ps_w[:, 0:NP], xnt_t[t % 2][:, c * 128:(c + 1) * 128],
                             pm_s[:, c, :], start=(c == 0), stop=(c == 7))
            return r
        scw = [("ac", mk["xevac"][t]), ("spm", 16)]
        scw.append(("ac", mk["d2p"][t - 1]) if t >= 1 else ("dv", m_tinv))
        mk["sco"][t] = S.op("tensor", scomm, waits=scw, inc="pe")
        mk["d2p"][t] = S.op("scalar", (lambda e, t=t: e.activation(
            smA[:, t * NP:(t + 1) * NP], ps_w[:, 0:NP], AF.Identity,
            bias=x2all[:, t:t + 1], scale=-2.0)),
            waits=[("pe", mk["sco"][t]), ("dv", mk["nmr"][t])], inc="ac")

    def a_cs(t):
        if not (0 <= t < NT):
            return

        def csmm(e, t=t):
            r = None
            for h in range(2):
                r = e.matmul(ps_cs[h][:, :], big_s[:, 256 + 16 * t:256 + 16 * (t + 1)], u_t[t % 2][:, H(h)],
                             start=(t == 0), stop=(t == NT - 1))
            return r
        csw = [("pl", mk["u"][t])]
        if t == 0:
            csw += [("ac", m_r16f2), ("ct", m_consts)]
        mk["cs"][t] = S.op("tensor", csmm, waits=csw, inc="pe")

    a_load(0)
    a_load(1)
    a_ln(0)
    for s in range(NT + 3):
        a_ln(s + 1)
        a_xt(s)
        a_xevac(s)
        a_kv(s - 1)
        a_u(s - 1)
        a_sco(s - 1)
        a_cs(s - 2)
        a_load(s + 2)

    def sh_mul(e):
        r = None
        for h in range(2):
            r = e.tensor_mul(sh_s[0:16, H(h)], ps_cs[h][0:16, :], r16_s[0:16, H(h)])
        return r
    m_sh = S.op("vector", sh_mul, waits=[("pe", mk["cs"][NT - 1]), ("ac", m_r16f2)], inc="dv")

    def spre(e):
        r = None
        for h in range(2):
            r = e.matmul(ps_a[h][0:1, :], ones128_s[0:16, 0:1], sh_s[0:16, H(h)], start=True, stop=True)
        return r
    m_spre = S.op("tensor", spre, waits=[("dv", m_sh), ("ac", mk["kevac"][NT - 1])], inc="pe")

    def srow_mul(e):
        r = None
        for h in range(2):
            r = e.tensor_mul(rowC[0:1, H(h)], ps_a[h][0:1, :], rowA[0:1, H(h)])
        return r
    m_srow = S.op("vector", srow_mul, waits=[("pe", m_spre), ("ac", m_d2047)], inc="dv")

    S.op("gpsimd", lambda e: e.dma_start(out=cc_in[:, :].opt(), in_=rowC[:, :]),
         waits=[("dv", m_srow)], inc="cw", inc_by=16)
    m_ccin = S.cnt["cw"]
    S.op("gpsimd", lambda e: e.collective_compute(
        "AllGather", ALU.bypass, replica_groups=[list(range(N_CORES))],
        ins=[cc_in.ap().opt()], outs=[cc_out.ap().opt()]),
        waits=[("cw", m_ccin)], inc="cc")
    S.op("gpsimd", lambda e: e.dma_start(out=sall_s[0:8, :], in_=cc_out[:, :].opt()),
         waits=[("cc", 1)], inc="cw", inc_by=16)
    m_sall = S.cnt["cw"]

    def gall_outer(e):
        r = None
        for h in range(2):
            r = e.matmul(ps_b[h][0:8, :], qneg_v, ed_s[0:1, H(h)], start=True, stop=True)
        return r
    m_go = S.op("tensor", gall_outer, waits=[("dv", mk["kv"][NT - 1])], inc="pe")
    m_gall = S.op("scalar", exp_evac(gall_s, ps_b, rows_=8), waits=[("pe", m_go)], inc="ac")

    def prefmm_a(e):
        r = None
        for h in range(2):
            r = e.matmul(ps_cs[h][:, :], lt16_s[:, :], sh_s[0:16, H(h)], start=True, stop=False)
        return r
    m_prefa = S.op("tensor", prefmm_a, waits=[("dv", m_sh), ("ct", m_consts)], inc="pe")

    from concourse.bass_types import AP as _AP
    G = NT
    W = G * NP
    smB, eqA = f2x, f4

    def v3(buf):
        return buf[:, 0:W].rearrange("p (g e) -> p g e", g=G)

    def gb(col):
        return _AP(col.tensor, col.offset, [col.ap[0], (col.ap[1][0], G), (0, NP)])

    def r_b1(e):
        p2c = p2bc[:, :]
        p2b3 = _AP(p2c.tensor, p2c.offset, [p2c.ap[0], (0, G), (p2c.ap[1][0], NP)])
        CH(e, e.tensor_tensor(v3(smB), v3(smA), p2b3, ALU.add))
        return e.tensor_scalar_max(smA[:, 0:W], smB[:, 0:W], 0.0)
    m_rb1 = S.op("vector", r_b1,
                 waits=[("ac", mk["d2p"][NT - 1]), ("pl", mk["u"][NT - 1]), ("dv", m_tinv)],
                 inc="dv")

    def r_b2(e):
        CH(e, e.tensor_reduce(sm1[:, 0:G], v3(smA), AX.X, ALU.min))
        CH(e, e.tensor_tensor(v3(eqA), v3(smA), gb(sm1[:, 0:G]), ALU.is_equal))
        CH(e, e.scalar_tensor_tensor(smB[:, 0:W], eqA[:, 0:W], 1e9, smA[:, 0:W], ALU.mult, ALU.add))
        CH(e, e.tensor_reduce(sm1[:, G:2 * G], v3(smB), AX.X, ALU.min))
        return e.tensor_tensor(v3(smA), v3(smB), gb(sm1[:, G:2 * G]), ALU.is_equal)
    m_rb2 = S.op("vector", r_b2, waits=[("dv", m_rb1)], inc="dv")

    def r_sq(e):
        CH(e, e.activation(sm2[:, :], sm1[:, :], AF.Ln), "chs")
        return e.activation(sm3[:, :], sm2[:, :], AF.Exp, scale=0.5)
    m_rsq = S.op("scalar", r_sq, waits=[("dv", m_rb2)], inc="ac")

    def r_dd(e):
        CH(e, e.tensor_sub(sm4[:, 0:G], sm3[:, G:2 * G], sm3[:, 0:G]))
        return e.tensor_scalar(sm4[:, G:2 * G], sm4[:, 0:G], tinvb[:, 0:1], None, ALU.mult)
    m_rdd = S.op("vector", r_dd, waits=[("ac", m_rsq)], inc="dv")
    m_rsg = S.op("scalar", lambda e: e.activation(sm5[:, 0:G], sm4[:, G:2 * G], AF.Exp, scale=-1.0),
                 waits=[("dv", m_rdd)], inc="ac")

    def r_w(e):
        CH(e, e.tensor_scalar_add(sm5[:, G:2 * G], sm5[:, 0:G], 1.0))
        CH(e, e.reciprocal(sm2[:, 0:G], sm5[:, G:2 * G]))
        CH(e, e.tensor_scalar(sm2[:, G:2 * G], sm2[:, 0:G], -1.0, 1.0, ALU.mult, ALU.add))
        CH(e, e.tensor_tensor(v3(smB), v3(smA), gb(sm2[:, 0:G]), ALU.mult))
        CH(e, e.tensor_tensor(v3(smA), v3(eqA), gb(sm2[:, G:2 * G]), ALU.mult))
        return e.tensor_add(wsall[:, :], smA[:, 0:W], smB[:, 0:W])
    m_route = S.op("vector", r_w, waits=[("ac", m_rsg)], inc="dv")
    m_prod = S.op("vector", lambda e: e.tensor_mul(prod8_s[0:8, :], gall_s[0:8, :], sall_s[0:8, :]),
                  waits=[("ac", m_gall), ("cw", m_sall)], inc="dv")

    def issue_b_tail():
        def ccol(e):
            r = None
            for h in range(2):
                r = e.matmul(ps_b[h][0:1, :], ones128_s[0:8, 0:1], prod8_s[0:8, H(h)],
                             start=True, stop=True)
            return r
        m_cc2 = S.op("tensor", ccol, waits=[("dv", m_prod)], inc="pe")

        def cr0_mul(e):
            r = None
            for h in range(2):
                r = e.tensor_mul(cr0_s[0:1, H(h)], ps_b[h][0:1, :], rowB[0:1, H(h)])
            return r
        m_cr0 = S.op("vector", cr0_mul, waits=[("pe", m_cc2), ("ac", m_d1)], inc="dv")

        def prefmm(e):
            r = None
            for h in range(2):
                r = e.matmul(ps_cs[h][:, :], ones1x16_v, cr0_s[0:1, H(h)], start=False, stop=True)
            return r
        m_pref = S.op("tensor", prefmm, waits=[("dv", m_cr0)], inc="pe")

        def cry_mul(e):
            r = None
            for h in range(2):
                r = e.tensor_mul(cr_s[0:16, H(h)], ps_cs[h][0:16, :], f2_s[0:16, H(h)])
            return r
        m_cry = S.op("vector", cry_mul, waits=[("pe", m_pref)], inc="dv")

        cs_ctx.close()
        po = [ctx.enter_context(nc.psum_tensor(f"ps_o{h}", [128, 512], F32)) for h in range(2)]
        return m_cr0, m_cry, po
    m_cr0 = m_cry = None
    ps_o = None

    c = dict(xC=[0] * NT, xnC=[0] * NT, uC=[0] * NT, wr=[0] * NT, rsig=[0] * NT,
             cum=[0] * NT, t1=[0] * NT, t2=[0] * NT, wkv=[0] * NT, rwkv=[0] * NT,
             rT=[0] * NT, rTev=[0] * NT, womm=[0] * NT, wse=[0] * NT,
             tapb=[0] * NT, tapw=[0] * NT, tapr=[0] * NT,
             hmm=[0] * NT, hev=[0] * NT, g=[0] * NT, gT=[0] * NT,
             gTev=[0] * NT, omm=[0] * NT, out=[0] * NT, yst=[0] * NT)
    out_t = [f5, f6]

    def c_load_xn(t):
        if not (0 <= t < NT) or c["xnC"][t]:
            return
        xnw = [("xs0", S.cnt["xs0"]), ("xs1", S.cnt["xs1"]), ("pe", mk["sco"][NT - 1])]
        if t >= 2:
            xnw.append(("pe", c["hmm"][t - 2]))
        c["xnC"][t] = S.op("sync", (lambda e, t=t: e.dma_start(
            out=xnt_t[t % 2][:, :], in_=xnt_hbm[t, :, :])), waits=xnw, inc="sn%d" % (t % 2), inc_by=16)

    def c_load(t):
        if not (0 <= t < NT) or c["uC"][t]:
            return
        c_load_xn(t)
        uw = [("uss0", S.cnt["uss0"]), ("uss1", S.cnt["uss1"]), ("pe", mk["cs"][NT - 1])]
        if t >= 2:
            uw += [("pe", c["cum"][t - 2]), ("pl", c["t2"][t - 2])]
        c["uC"][t] = S.op("sync", (lambda e, t=t: e.dma_start(
            out=u_t[t % 2][:, :], in_=u_hbm[t, :, :])), waits=uw, inc="su%d" % (t % 2), inc_by=16)
        xw = [("dv", mk["xn"][NT - 1]), ("ac", mk["sq"][NT - 1])]
        if t >= 2:
            xw.append(("dv", c["out"][t - 2]))
        c["xC"][t] = S.op("sync", (lambda e, t=t: e.dma_start(
            out=x_t[t % 2][:, :], in_=x_h[t * 128:(t + 1) * 128, :])),
            waits=xw, inc="sx%d" % (t % 2), inc_by=16)

    def c_rT(t):
        if not (0 <= t < NT):
            return
        rtw = [("dv", c["rwkv"][t])]
        if t == 0:
            rtw += [("ac", c["gTev"][2]), ("ac", c["wTev"][2])]
        elif t == 1:
            rtw += [("ac", c["rTev"][0]), ("ac", c["wTev"][1])]
        elif t == 2:
            rtw += [("ac", c["rTev"][1]), ("ac", c["gTev"][2]), ("ac", c["wTev"][2])]
        elif t == 3:
            rtw += [("ac", c["rTev"][2]), ("ac", c["wTev"][3])]
        else:
            rtw += [("ac", c["gTev"][t - 1]), ("ac", c["wTev"][t])]

        def rtrans(e, t=t):
            r = None
            for cc_ in range(8):
                r = e.transpose(ps_t[:, cc_ * 128:(cc_ + 1) * 128],
                                rwkv_t[t % 2][:, cc_ * 128:(cc_ + 1) * 128], id128_v)
            return r
        c["rT"][t] = S.op("tensor", rtrans, waits=rtw, inc="pe")
        rtev = [("pe", c["rT"][t])]
        if t >= 2:
            rtev.append(("pe", c["womm"][t - 2]))
        c["rTev"][t] = S.op("scalar", (lambda e, t=t: e.copy(rwkvT_t[t % 2][:, :], ps_t[:, :])),
                            waits=rtev, inc="ac")

    def c_h(t):
        if not (0 <= t < NT) or c["hmm"][t]:
            return

        def hmm(e, t=t):
            r = None
            for cc_ in range(8):
                r = e.matmul(ps_a[0][:, :], xnt_t[t % 2][:, cc_ * 128:(cc_ + 1) * 128],
                             b_s[:, cc_, :], start=(cc_ == 0), stop=(cc_ == 7))
            return r
        c["hmm"][t] = S.op("tensor", hmm, waits=[("ac", c["rsig"][t])], inc="pe")
        hev = [("pe", c["hmm"][t])]
        if t >= 1:
            hev.append(("dv", c["g"][t - 1]))
        c["hev"][t] = S.op("scalar", lambda e: e.copy(h_s[:, :], ps_a[0][:, :]),
                           waits=hev, inc="ac")

    def c_g(t):
        if not (0 <= t < NT) or c["g"][t]:
            return
        gw = [("ac", c["hev"][t])]
        if t >= 1:
            gw.append(("pe", c["gT"][t - 1]))

        def gmul(e, t=t):
            wv = wsall[:, t * NP:(t + 1) * NP]
            from concourse.bass_types import AP as _AP
            bc = _AP(wv.tensor, wv.offset, [wv.ap[0], (1, NP), (0, RK)])
            return e.tensor_tensor(g_bf[:, :].rearrange("p (e r) -> p e r", e=NP),
                                   h_s[:, :].rearrange("p (e r) -> p e r", e=NP),
                                   bc, ALU.mult)
        c["g"][t] = S.op("vector", gmul, waits=gw, inc="dv")

    def c_wsT(t):
        if not (0 <= t < NT) or c["wse"][t]:
            return
        if t == 0:
            ww = [("dv", m_route), ("ac", mk["xevac"][NT - 1])]
        elif t in (1, 2):
            ww = [("dv", m_route), ("ac", c["wTev"][t - 1])]
        else:
            ww = [("dv", m_route), ("ac", c["rTev"][t - 1])]
        c["wse"][t] = S.op("tensor", (lambda e, t=t: e.transpose(
            ps_t[0:NP, 512:640], wsall[:, t * NP:(t + 1) * NP], id128_v)),
            waits=ww, inc="pe")
        wtw = [("pe", c["wse"][t])]
        if t >= 3:
            wtw.append(("pe", c["omm"][t - 3]))
        c["wTev"] = c.get("wTev", [0] * NT)
        c["wTev"][t] = S.op("scalar", (lambda e, t=t: e.copy(
            wselT_t[t % 3][:, :], ps_t[0:NP, 512:640])), waits=wtw, inc="ac")

    def c_wr(t):
        if not (0 <= t < NT) or c["wr"][t]:
            return
        wrw = [("sn%d" % (t % 2), c["xnC"][t]), ("wt", m_wall)]
        if t >= 1:
            wrw += [("ac", c["hev"][t - 1]), ("ac", c["rsig"][t - 1])]
        else:
            wrw.append(("dv", m_srow))
        c["wr"][t] = S.op("tensor", proj(ps_a, wr_s, xnt_t[t % 2][:, :]), waits=wrw, inc="pe")

        def rsig(e, t=t):
            r = None
            for h in range(2):
                r = e.activation(rb_t[t % 3][:, H(h)], ps_a[h][:, :], AF.Sigmoid)
            return r
        rw = [("pe", c["wr"][t])]
        if t >= 3:
            rw.append(("dv", c["rwkv"][t - 3]))
            if _DEBUG_RWKV:
                rw.append(("dbg", c["tapb"][t - 3]))
        c["rsig"][t] = S.op("scalar", rsig, waits=rw, inc="ac")

    def c_scan(t):
        if not (0 <= t < NT):
            return

        def cummm(e, t=t):
            r = None
            for h in range(2):
                e.matmul(ps_b[h][:, :], lt128_v, u_t[t % 2][:, H(h)], start=True, stop=False)
                r = e.matmul(ps_b[h][:, :], ohp_s[0:16, 128 * t:128 * (t + 1)], cr_s[0:16, H(h)], start=False, stop=True)
            return r
        cw = [("su%d" % (t % 2), c["uC"][t]), ("dv", m_cry)]
        cw.append(("dv", c["t1"][t - 1]) if t >= 1 else ("ac", m_gall))
        if t == 0:
            cw.append(("ct", m_consts))
        c["cum"][t] = S.op("tensor", cummm, waits=cw, inc="pe")

        def t1mul(e):
            r = None
            for h in range(2):
                r = e.tensor_mul(f2x[:, H(h)], ps_b[h][:, :], dm1_s[:, H(h)])
            return r
        c["t1"][t] = S.op("vector", t1mul, waits=[("pe", c["cum"][t]), ("ac", m_dm1)], inc="dv")

        t2w = [("su%d" % (t % 2), c["uC"][t]), ("ac", m_eud)]
        if t >= 1:
            t2w.append(("dv", c["wkv"][t - 1]))
        c["t2"][t] = S.op("gpsimd", (lambda e, t=t: e.tensor_tensor(
            f3[:, :], u_t[t % 2][:, :], eud_s[:, :], ALU.mult)), waits=t2w, inc="pl")

        wkvw = [("pl", c["t2"][t]), ("dv", c["t1"][t])]
        if _DEBUG_RWKV and t >= 1:
            wkvw.append(("dbg", c["tapw"][t - 1]))
        c["wkv"][t] = S.op("vector", lambda e: e.tensor_add(f4[:, :], f2x[:, :], f3[:, :]),
                           waits=wkvw, inc="dv")
        rkw = [("ac", c["rsig"][t]), ("dv", c["wkv"][t])]
        if t >= 1:
            rkw.append(("pe", c["rT"][t - 1]))
        if _DEBUG_RWKV and t >= 2:
            rkw.append(("dbg", c["tapr"][t - 2]))
        c["rwkv"][t] = S.op("vector", (lambda e, t=t: e.tensor_mul(
            rwkv_t[t % 2][:, :], f4[:, :], rb_t[t % 3][:, :])), waits=rkw, inc="dv")
        if _DEBUG_RWKV:
            c["tapw"][t] = S.op("gpsimd", (lambda e, t=t: e.dma_start(
                out=dbg_wkv[t, :, :], in_=f4[:, :])),
                waits=[("dv", c["wkv"][t])], inc="dbg", inc_by=16)
            c["tapb"][t] = S.op("gpsimd", (lambda e, t=t: e.dma_start(
                out=dbg_rb[t, :, :], in_=rb_t[t % 3][:, :])),
                waits=[("ac", c["rsig"][t])], inc="dbg", inc_by=16)
            c["tapr"][t] = S.op("gpsimd", (lambda e, t=t: e.dma_start(
                out=dbg_rwkv[t, :, :], in_=rwkv_t[t % 2][:, :])),
                waits=[("dv", c["rwkv"][t])], inc="dbg", inc_by=16)

    def c_womm(t):
        if not (0 <= t < NT):
            return

        def womm(e, t=t):
            r = None
            for h in range(2):
                for cc_ in range(8):
                    r = e.matmul(ps_o[h][:, :], rwkvT_t[t % 2][:, cc_ * 128:(cc_ + 1) * 128],
                                 wo_s[:, cc_, H(h)], start=(cc_ == 0), stop=False)
            return r
        wow = [("ac", c["rTev"][t])]
        if t >= 1:
            wow.append(("dv", c["out"][t - 1]))
        c["womm"][t] = S.op("tensor", womm, waits=wow, inc="pe")

    def c_gT(t):
        if not (0 <= t < NT) or c["gT"][t]:
            return

        def gtrans(e):
            r = None
            for m in range(4):
                r = e.transpose(ps_t[:, m * 128:(m + 1) * 128], g_bf[:, m * 128:(m + 1) * 128], id128_v)
            return r
        if t == 0:
            gw2 = [("dv", c["g"][t]), ("ac", mk["xevac"][NT - 1])]
        elif t in (1, 2):
            gw2 = [("dv", c["g"][t]), ("ac", c["gTev"][t - 1])]
        else:
            gw2 = [("dv", c["g"][t]), ("ac", c["rTev"][t])]
        c["gT"][t] = S.op("tensor", gtrans, waits=gw2, inc="pe")
        gtev = [("pe", c["gT"][t])]
        if t >= 3:
            gtev.append(("pe", c["omm"][t - 3]))
        c["gTev"][t] = S.op("scalar", (lambda e, t=t: e.copy(gT_t[t % 3][:, :], ps_t[:, 0:ER])),
                            waits=gtev, inc="ac")

    def c_am(t):
        if not (0 <= t < NT):
            return

        def ammm(e, t=t):
            r = None
            for h in range(2):
                for m in range(4):
                    e.matmul(ps_o[h][:, :], gT_t[t % 3][:, m * 128:(m + 1) * 128],
                             a_s[:, m, H(h)], start=False, stop=False)
                r = e.matmul(ps_o[h][:, :], wselT_t[t % 3][:, :], bias_s[:, H(h)],
                             start=False, stop=True)
            return r
        omw = [("ac", c["gTev"][t]), ("ac", c["wTev"][t])]
        if t < 2:
            omw.append(("dv", m_cry))
        c["omm"][t] = S.op("tensor", ammm, waits=omw, inc="pe")

        def outadd(e, t=t):
            r = None
            for h in range(2):
                r = e.tensor_add(out_t[t % 2][:, H(h)], ps_o[h][:, :], x_t[t % 2][:, H(h)])
            return r
        ow = [("pe", c["omm"][t]), ("sx%d" % (t % 2), c["xC"][t])]
        if t >= 2:
            ow.append(("ys%d" % (t % 2), c["yst"][t - 2]))
        c["out"][t] = S.op("vector", outadd, waits=ow, inc="dv")
        c["yst"][t] = S.op("scalar", (lambda e, t=t: e.dma_start(
            out=y_h[t * 128:(t + 1) * 128, :], in_=out_t[t % 2][:, :])),
            waits=[("dv", c["out"][t])], inc="ys%d" % (t % 2), inc_by=16)

    c_load(0)
    c_load(1)
    c_wr(0)
    c_h(0)
    c_wsT(0)
    c_g(0)
    c_wr(1)
    c_gT(0)
    c_h(1)
    c_wsT(1)
    c_g(1)
    c_gT(1)
    c_load_xn(2)
    c_wr(2)
    c_h(2)
    c_wsT(2)
    c_g(2)
    c_gT(2)
    c_load_xn(3)

    m_cr0, m_cry, ps_o = issue_b_tail()

    for s in range(NT + 2):
        c_wsT(s - 1)
        c_h(s - 1)
        c_rT(s - 1)
        c_g(s - 1)
        c_wr(s)
        c_gT(s - 1)
        c_scan(s)
        c_womm(s - 1)
        c_am(s - 1)
        c_load(s + 1)

    if _DEBUG_TAPS:
        taps_sb = {
            "ed": ed_s, "sh": sh_s, "cr": cr_s, "cr0": cr0_s, "x2": x2all,
            "rows": rows, "f1": f1, "f2x": f2x, "f3": f3, "f4": f4,
            "p2bc": p2bc, "tinvb": tinvb, "sm1": sm1, "sm3": sm3, "sm4": sm4,
            "dneg": dneg_s, "dm1": dm1_s, "eud": eud_s, "r16": r16_s, "f2c": f2_s,
            "hs": h_s, "gbf": g_bf, "rowC": rowC, "mrow": mrow, "wtall": wtall,
        }
        taps_dram = {"u": (u_hbm, [NT, 128, D], BF16), "xnt": (xnt_hbm, [NT, 128, D], BF16)}
        endw = [("ys0", S.cnt["ys0"]), ("ys1", S.cnt["ys1"]), ("dv", S.cnt["dv"]),
                ("ac", S.cnt["ac"]), ("pe", S.cnt["pe"]), ("pl", S.cnt["pl"])]
        for name in _DEBUG_TAPS:
            if name in taps_sb:
                t = taps_sb[name]
                o = nc.dram_tensor("dbg_" + name, list(t.shape), t.dtype, kind="ExternalOutput").ap()
                S.op("sync", (lambda e, t=t, o=o: e.dma_start(
                    out=o, in_=t[tuple(slice(None) for _ in t.shape)])), waits=endw, inc="dbg", inc_by=16)
            else:
                src_, shp, dt = taps_dram[name]
                o = nc.dram_tensor("dbg_" + name, shp, dt, kind="ExternalOutput").ap()
                S.op("sync", (lambda e, s=src_, o=o: e.dma_start(out=o, in_=s)),
                     waits=endw, inc="dbg", inc_by=16)
        S.op("sync", lambda e: e.nop(), waits=[("dbg", S.cnt["dbg"])])

    S.op("scalar", lambda e: e.nop(), waits=[("ys0", S.cnt["ys0"]), ("ys1", S.cnt["ys1"])])

    with nc.Block() as block:
        @block.sync
        def _(e):
            S.run(e, "sync", sems)

        @block.scalar
        def _(e):
            S.run(e, "scalar", sems)

        @block.vector
        def _(e):
            S.run(e, "vector", sems)

        @block.tensor
        def _(e):
            S.run(e, "tensor", sems)

        @block.gpsimd
        def _(e):
            S.run(e, "gpsimd", sems)

    sem_ctx.close()
    ctx.close()
    return nc


def _consts():
    i_ = np.arange(128, dtype=np.float32)
    lt128 = (i_[:, None] < i_[None, :]).astype(ml_dtypes.bfloat16)
    id128 = np.eye(128, dtype=ml_dtypes.bfloat16)
    g16 = np.arange(16, dtype=np.float32)
    lt16 = (g16[:, None] < g16[None, :]).astype(np.float32)
    eg = np.zeros((128, NT, 16), dtype=ml_dtypes.bfloat16)
    for g in range(NT):
        eg[:, g, g] = 1.0
    oh16 = np.zeros((16, NT, 128), dtype=ml_dtypes.bfloat16)
    for g in range(NT):
        oh16[g, g, :] = 1.0
    emat = np.zeros((NP, ER), dtype=ml_dtypes.bfloat16)
    for e in range(NP):
        emat[e, e * RK:(e + 1) * RK] = 1.0
    bigc = np.concatenate([lt128, id128, eg.reshape(128, NT * 16)], axis=1)
    crow_tail = np.zeros((1, 551), np.float32)
    crow_tail[0, 7:135] = i_
    crow_tail[0, 135:263] = -i_
    crow_tail[0, 263:391] = i_ - 1.0
    crow_tail[0, 391:407] = g16 * 128.0
    crow_tail[0, 407:423] = 1.0
    crow_tail[0, 423:551] = 1.0
    return {
        "bigc": np.ascontiguousarray(bigc),
        "ohp": np.ascontiguousarray(oh16.reshape(16, NT * 128)),
        "lt16": lt16,
        "ones128": np.ones((128, 1), np.float32),
        "emat": emat,
        "_crow_tail": crow_tail,
    }


def kernel(L0_attn_x, L0_attn_Wk, L0_attn_Wv, L0_attn_Wr, L0_attn_Wo,
           L0_attn_time_decay, L0_attn_time_first,
           L0_ffn_prototypes, L0_ffn_B, L0_ffn_A, L0_ffn_bias, L0_ffn_temp):
    x = np.ascontiguousarray(np.asarray(L0_attn_x, np.float32).reshape(SEQ, D))
    shard = [np.ascontiguousarray(x[cix * TPC:(cix + 1) * TPC]) for cix in range(N_CORES)]
    common = {
        "wk": np.ascontiguousarray(np.asarray(L0_attn_Wk, np.float32).T),
        "wv": np.ascontiguousarray(np.asarray(L0_attn_Wv, np.float32).T),
        "wr": np.ascontiguousarray(np.asarray(L0_attn_Wr, np.float32).T),
        "wo": np.ascontiguousarray(np.asarray(L0_attn_Wo, np.float32).T),
        "bm": np.ascontiguousarray(np.asarray(L0_ffn_B, np.float32).transpose(2, 0, 1).reshape(D, ER)),
        "am": np.ascontiguousarray(np.asarray(L0_ffn_A, np.float32).transpose(0, 2, 1).reshape(ER, D)),
        "pm": np.ascontiguousarray(np.asarray(L0_ffn_prototypes, np.float32).T),
        "bias": np.ascontiguousarray(np.asarray(L0_ffn_bias, np.float32)),
    }
    common.update(_consts())
    crow_tail = common.pop("_crow_tail")
    crow_base = np.zeros((1, 2608), np.float32)
    crow_base[0, 0:1024] = np.asarray(L0_attn_time_decay, np.float32).reshape(-1)
    crow_base[0, 1024:2048] = np.asarray(L0_attn_time_first, np.float32).reshape(-1)
    crow_base[0, 2048] = np.asarray(L0_ffn_temp, np.float32).reshape(-1)[0]
    crow_base[0, 2057:2608] = crow_tail[0, :]
    in_maps = []
    for cix in range(N_CORES):
        qneg = np.full(N_CORES, -1e9, np.float32)
        for k in range(cix):
            qneg[k] = -2048.0 * (cix - 1 - k)
        crow = crow_base.copy()
        crow[0, 2049:2049 + N_CORES] = qneg
        m = dict(common)
        m["x"] = shard[cix]
        m["crow"] = crow
        in_maps.append(m)
    nc = build_program()
    res = run_bass_kernel_spmd(nc, in_maps, core_ids=list(range(N_CORES)))
    y = np.concatenate([res.results[cix]["y"] for cix in range(N_CORES)], axis=0)
    return y.reshape(1, SEQ, D)


# version 30
# speedup vs baseline: 1.0542x; 1.0542x over previous
import numpy as np
import ml_dtypes
import concourse.bass as bass
import concourse.mybir as mybir
from concourse.bass_utils import run_bass_kernel_spmd

F32 = mybir.dt.float32
BF16 = mybir.dt.bfloat16
AF = mybir.ActivationFunctionType
ALU = mybir.AluOpType
AX = mybir.AxisListType

N_CORES = 8
SEQ = 16384
D = 1024
TPC = SEQ // N_CORES
NT = TPC // 128
NP = 32
RK = 16
ER = NP * RK
EPS = 1e-5

ENGS = ("sync", "scalar", "vector", "tensor", "gpsimd")

_DEBUG_TAPS = ()
_DEBUG_RWKV = False


class Sched:

    def __init__(self):
        self.steps = {e: [] for e in ENGS}
        from collections import defaultdict
        self.cnt = defaultdict(int)

    def op(self, eng, emit, waits=(), inc=None, inc_by=1):
        waits = tuple((s, v) for (s, v) in waits if v > 0)
        self.steps[eng].append((emit, waits, inc, inc_by))
        if inc is not None:
            self.cnt[inc] += inc_by
            return self.cnt[inc]
        return None

    def run(self, e, eng, sems):
        lastw = {}
        for emit, waits, inc, inc_by in self.steps[eng]:
            for s, v in waits:
                if lastw.get(s, -1) < v:
                    e.wait_ge(sems[s], v)
                    lastw[s] = v
            inst = emit(e)
            if inc is not None:
                assert inst is not None
                inst.then_inc(sems[inc], inc_by)


def build_program():
    nc = bass.Bass("TRN2", target_bir_lowering=False, debug=False)

    x_h = nc.dram_tensor("x", [TPC, D], F32, kind="ExternalInput").ap()
    wk_h = nc.dram_tensor("wk", [D, D], F32, kind="ExternalInput").ap()
    wv_h = nc.dram_tensor("wv", [D, D], F32, kind="ExternalInput").ap()
    wr_h = nc.dram_tensor("wr", [D, D], F32, kind="ExternalInput").ap()
    wo_h = nc.dram_tensor("wo", [D, D], F32, kind="ExternalInput").ap()
    bm_h = nc.dram_tensor("bm", [D, ER], F32, kind="ExternalInput").ap()
    am_h = nc.dram_tensor("am", [ER, D], F32, kind="ExternalInput").ap()
    pm_h = nc.dram_tensor("pm", [D, NP], F32, kind="ExternalInput").ap()
    bias_h = nc.dram_tensor("bias", [NP, D], F32, kind="ExternalInput").ap()
    crow_h = nc.dram_tensor("crow", [1, 2608], F32, kind="ExternalInput").ap()
    bigc_h = nc.dram_tensor("bigc", [128, 512], BF16, kind="ExternalInput").ap()
    ohp_h = nc.dram_tensor("ohp", [16, NT * 128], BF16, kind="ExternalInput").ap()
    lt16_h = nc.dram_tensor("lt16", [16, 16], F32, kind="ExternalInput").ap()
    ones128_h = nc.dram_tensor("ones128", [128, 1], F32, kind="ExternalInput").ap()
    emat_h = nc.dram_tensor("emat", [NP, ER], BF16, kind="ExternalInput").ap()

    y_h = nc.dram_tensor("y", [TPC, D], F32, kind="ExternalOutput").ap()

    dbg_rwkv = (nc.dram_tensor("dbg_rwkv", [NT, 128, D], BF16, kind="ExternalOutput").ap()
                if _DEBUG_RWKV else None)
    dbg_wkv = (nc.dram_tensor("dbg_wkv", [NT, 128, D], F32, kind="ExternalOutput").ap()
               if _DEBUG_RWKV else None)
    dbg_rb = (nc.dram_tensor("dbg_rb", [NT, 128, D], BF16, kind="ExternalOutput").ap()
              if _DEBUG_RWKV else None)
    xnt_hbm = nc.dram_tensor("xnt_hbm", [NT, 128, D], BF16, kind="Internal").ap()
    u_hbm = nc.dram_tensor("u_hbm", [NT, 128, D], BF16, kind="Internal").ap()
    cc_in = nc.dram_tensor("cc_in", [1, D], F32, kind="Internal")
    cc_out = nc.dram_tensor("cc_out", [8, D], F32, kind="Internal")
    cc_dout = nc.dram_tensor("cc_dout", [8, D], F32, kind="Internal")

    from contextlib import ExitStack
    ctx = ExitStack()
    sb = lambda name, shape, dt: ctx.enter_context(nc.sbuf_tensor(name, shape, dt))

    wk_s = sb("wk_s", [128, 8, D], BF16)
    wv_s = sb("wv_s", [128, 8, D], BF16)
    wr_s = sb("wr_s", [128, 8, D], BF16)
    wo_s = sb("wo_s", [128, 8, D], BF16)
    b_s = sb("b_s", [128, 8, ER], BF16)
    a_s = sb("a_s", [128, 4, D], BF16)
    pm_s = sb("pm_s", [128, 8, NP], BF16)
    bias_s = sb("bias_s", [NP, D], BF16)

    big_s = sb("big_s", [128, 512], BF16)
    ohp_s = sb("ohp_s", [16, NT * 128], BF16)
    lt16_s = sb("lt16_s", [16, 16], F32)
    ones128_s = sb("ones128_s", [128, 1], F32)
    crow_s = sb("crow_s", [1, 2608], F32)
    lt128_v = big_s[:, 0:128]
    id128_v = big_s[:, 128:256]
    td_v = crow_s[0:1, 0:1024]
    tf_v = crow_s[0:1, 1024:2048]
    temp_v = crow_s[0:1, 2048:2049]
    qneg_v = crow_s[0:1, 2049:2057]
    iota_v = crow_s[0:1, 2064:2192]
    niota_v = crow_s[0:1, 2192:2320]
    iotam1_v = crow_s[0:1, 2320:2448]
    iota16_v = crow_s[0:1, 2448:2464]
    ones1x16_v = crow_s[0:1, 2464:2480]
    ones1x128_v = crow_s[0:1, 2480:2608]
    ed_s = sb("ed_s", [1, D], F32)
    rowA = sb("rowA", [1, D], F32)
    rowB = sb("rowB", [1, D], F32)
    rowC = sb("rowC", [1, D], F32)
    dneg_s = sb("dneg_s", [128, D], F32)
    dm1_s = sb("dm1_s", [128, D], F32)
    eud_s = sb("eud_s", [128, D], F32)
    r16_s = sb("r16_s", [16, D], F32)
    f2_s = sb("f2_s", [16, D], F32)
    sh_s = sb("sh_s", [16, D], F32)
    cr_s = sb("cr_s", [16, D], BF16)
    cr0_s = sb("cr0_s", [1, D], F32)
    x2all = sb("x2all", [128, NT], F32)
    p2row = sb("p2row", [1, NP], F32)
    p2bc = sb("p2bc", [128, NP], F32)
    tinvb = sb("tinvb", [128, 1], F32)
    rows = sb("rows", [128, 40], F32)
    mrow = sb("mrow", [1, 8], F32)
    wsall = sb("wsall", [128, NT * NP], BF16)
    xa = sb("xa", [128, D], F32)
    xb = sb("xb", [128, D], F32)
    f1 = sb("f1", [128, D], F32)
    f2x = sb("f2x", [128, D], F32)
    f3 = sb("f3", [128, D], F32)
    f4 = sb("f4", [128, D], F32)
    f5 = sb("f5", [128, D], F32)
    f6 = sb("f6", [128, D], F32)
    x_t = [xa, xb]
    sall_s = f1
    gall_s = f5
    prod8_s = f6
    xn0 = sb("xn0", [128, D], BF16)
    xn1 = sb("xn1", [128, D], BF16)
    xnt0 = sb("xnt0", [128, D], BF16)
    xnt1 = sb("xnt1", [128, D], BF16)
    u0 = sb("u0", [128, D], BF16)
    u1 = sb("u1", [128, D], BF16)
    rwkvT0 = sb("rwkvT0", [128, D], BF16)
    rwkvT1 = sb("rwkvT1", [128, D], BF16)
    xn_t = [xn0, xn1]
    xnt_t = [xnt0, xnt1]
    u_t = [u0, u1]
    rwkvT_t = [rwkvT0, rwkvT1]
    rwkv_t = [xn0, xn1]
    h_s = sb("h_s", [128, ER], BF16)
    g_bf = sb("g_bf", [128, ER], BF16)
    gt0 = sb("gt0", [128, ER], BF16)
    gt1 = sb("gt1", [128, ER], BF16)
    gt2 = sb("gt2", [128, ER], BF16)
    gT_t = [gt0, gt1, gt2]
    wt0 = sb("wt0", [NP, 128], BF16)
    wt1 = sb("wt1", [NP, 128], BF16)
    wt2 = sb("wt2", [NP, 128], BF16)
    wselT_t = [wt0, wt1, wt2]
    sm1 = sb("sm1", [128, NP], F32)
    sm2 = sb("sm2", [128, NP], F32)
    sm3 = sb("sm3", [128, NP], F32)
    sm4 = sb("sm4", [128, NP], F32)
    sm5 = sb("sm5", [128, NP], F32)
    smA = sb("smA", [128, NT * NP], F32)
    rb0 = sb("rb0", [128, D], BF16)
    rb1 = sb("rb1", [128, D], BF16)
    rb2 = sb("rb2", [128, D], BF16)
    rb_t = [rb0, rb1, rb2]


    ps_w = ctx.enter_context(nc.psum_tensor("ps_w", [128, 512], F32))
    ps_a = [ctx.enter_context(nc.psum_tensor(f"ps_a{h}", [128, 512], F32)) for h in range(2)]
    ps_b = [ctx.enter_context(nc.psum_tensor(f"ps_b{h}", [128, 512], F32)) for h in range(2)]
    ps_t = ctx.enter_context(nc.psum_tensor("ps_t", [128, 1024], BF16))
    cs_ctx = ExitStack()
    ps_cs = [cs_ctx.enter_context(nc.psum_tensor(f"ps_cs{h}", [16, 512], F32)) for h in range(2)]

    sem_ctx = ExitStack()
    sems = {s: sem_ctx.enter_context(nc.semaphore(f"s_{s}")) for s in
            ("pe", "dv", "ac", "pl", "cc", "chv", "chs", "chp",
             "ct", "swk", "swv", "spm", "wt", "cw", "dbg",
             "sx0", "sx1", "sn0", "sn1", "su0", "su1",
             "xs0", "xs1", "uss0", "uss1", "ys0", "ys1")}

    S = Sched()
    H = lambda h: slice(h * 512, (h + 1) * 512)

    ccnt = {"chv": 0, "chs": 0, "chp": 0}

    def CH(e, inst, which="chv"):
        ccnt[which] += 1
        inst.then_inc(sems[which], 1)
        e.wait_ge(sems[which], ccnt[which])
        return inst

    S.op("sync", lambda e: e.dma_start(out=crow_s[:, :], in_=crow_h), inc="ct", inc_by=16)
    S.op("sync", lambda e: e.dma_start(out=ones128_s[:, :], in_=ones128_h), inc="ct", inc_by=16)
    m_cearly = S.cnt["ct"]
    S.op("sync", lambda e: e.dma_start(out=big_s[:, :], in_=bigc_h), inc="ct", inc_by=16)
    S.op("sync", lambda e: e.dma_start(out=ohp_s[:, :], in_=ohp_h), inc="ct", inc_by=16)
    S.op("sync", lambda e: e.dma_start(out=lt16_s[:, :], in_=lt16_h), inc="ct", inc_by=16)
    m_consts = S.cnt["ct"]

    def wload(dst, src, p=128):
        return lambda e: e.dma_start(out=dst[:, :, :], in_=src.rearrange("(c p) e -> p c e", p=p))
    S.op("gpsimd", wload(pm_s, pm_h), inc="spm", inc_by=16)
    S.op("gpsimd", wload(wk_s, wk_h), inc="swk", inc_by=16)
    S.op("gpsimd", wload(wv_s, wv_h), inc="swv", inc_by=16)
    S.op("gpsimd", wload(wr_s, wr_h), inc="wt", inc_by=16)
    S.op("gpsimd", wload(wo_s, wo_h), inc="wt", inc_by=16)
    S.op("gpsimd", wload(b_s, bm_h), inc="wt", inc_by=16)
    S.op("gpsimd", wload(a_s, am_h), inc="wt", inc_by=16)
    S.op("gpsimd", lambda e: e.dma_start(out=bias_s[:, :], in_=bias_h), inc="wt", inc_by=16)
    m_wall = S.cnt["wt"]
    S.op("gpsimd", lambda e: e.collective_compute(
        "AllGather", ALU.bypass, replica_groups=[list(range(N_CORES))],
        ins=[cc_in.ap().opt()], outs=[cc_dout.ap().opt()]), inc="cc")

    S.op("scalar", lambda e: e.activation(ed_s[:, :], td_v, AF.Exp),
         waits=[("ct", m_cearly)], inc="ac")
    m_ed = S.cnt["ac"]
    S.op("scalar", lambda e: e.activation(rowA[:, :], ed_s[:, :], AF.Exp, scale=-2047.0),
         waits=[("ac", m_ed)], inc="ac")
    m_d2047 = S.cnt["ac"]
    S.op("scalar", lambda e: e.activation(rowB[:, :], ed_s[:, :], AF.Exp, scale=-1.0), inc="ac")
    m_d1 = S.cnt["ac"]

    def outer2(ps, lhs_ap, rhs_row):
        def f(e):
            r = None
            for h in range(2):
                r = e.matmul(ps[h][:, :], lhs_ap, rhs_row[0:1, H(h)], start=True, stop=True)
            return r
        return f

    def exp_evac(dst, ps, scale=1.0, rows_=128):
        def f(e):
            r = None
            for h in range(2):
                r = e.activation(dst[0:rows_, H(h)], ps[h][0:rows_, :], AF.Exp, scale=scale)
            return r
        return f

    S.op("tensor", outer2(ps_a, iota_v, ed_s), waits=[("ac", m_ed), ("ct", m_cearly)], inc="pe")
    m_o1 = S.cnt["pe"]
    S.op("scalar", exp_evac(dneg_s, ps_a), waits=[("pe", m_o1)], inc="ac")
    m_dneg = S.cnt["ac"]
    S.op("tensor", outer2(ps_b, iotam1_v, ed_s), inc="pe")
    m_o2 = S.cnt["pe"]
    S.op("scalar", exp_evac(dm1_s, ps_b, scale=-1.0), waits=[("pe", m_o2)], inc="ac")
    m_dm1 = S.cnt["ac"]

    def eud_outer(e):
        r = None
        for h in range(2):
            e.matmul(ps_a[h][:, :], niota_v, ed_s[0:1, H(h)], start=True, stop=False)
            r = e.matmul(ps_a[h][:, :], ones1x128_v, crow_s[0:1, 1024 + 512 * h:1024 + 512 * (h + 1)], start=False, stop=True)
        return r
    S.op("tensor", eud_outer, waits=[("ac", m_dneg)], inc="pe")
    m_o3 = S.cnt["pe"]
    S.op("scalar", exp_evac(eud_s, ps_a), waits=[("pe", m_o3)], inc="ac")
    m_eud = S.cnt["ac"]

    def r16_outer(e):
        r = None
        for h in range(2):
            r = e.matmul(ps_cs[h][:, :], iota16_v, ed_s[0:1, H(h)], start=True, stop=True)
        return r
    S.op("tensor", r16_outer, inc="pe")
    m_o4 = S.cnt["pe"]

    def r16f2_evac(e):
        r = None
        for h in range(2):
            e.activation(r16_s[0:16, H(h)], ps_cs[h][0:16, :], AF.Exp)
            r = e.activation(f2_s[0:16, H(h)], ps_cs[h][0:16, :], AF.Exp, scale=-1.0)
        return r
    S.op("scalar", r16f2_evac, waits=[("pe", m_o4)], inc="ac")
    m_r16f2 = S.cnt["ac"]

    S.op("scalar", lambda e: e.activation(f4[:, 0:8 * NP], pm_s[:, :, :].rearrange("p c e -> p (c e)"), AF.Square),
         waits=[("spm", 16)], inc="ac")
    m_sqp = S.cnt["ac"]

    def p2_colsum(e):
        r = None
        for c in range(8):
            r = e.matmul(ps_w[0:1, 0:NP], ones128_s[:, :], f4[:, c * NP:(c + 1) * NP],
                         start=(c == 0), stop=(c == 7))
        return r
    S.op("tensor", p2_colsum, waits=[("ac", m_sqp)], inc="pe")
    m_p2 = S.cnt["pe"]
    S.op("vector", lambda e: e.tensor_copy(p2row[:, :], ps_w[0:1, 0:NP]), waits=[("pe", m_p2)], inc="dv")
    m_p2row = S.cnt["dv"]
    S.op("tensor", lambda e: e.matmul(ps_w[:, 0:NP], ones1x128_v, p2row[:, :], start=True, stop=True),
         waits=[("dv", m_p2row)], inc="pe")
    m_p2b = S.cnt["pe"]
    S.op("vector", lambda e: e.tensor_copy(p2bc[:, :], ps_w[:, 0:NP]), waits=[("pe", m_p2b)], inc="dv")
    m_p2bc = S.cnt["dv"]

    S.op("scalar", lambda e: e.activation(mrow[0:1, 0:1], temp_v, AF.Abs), inc="ac")
    m_t1 = S.cnt["ac"]

    def temp_chain(e):
        CH(e, e.tensor_scalar_max(mrow[0:1, 1:2], mrow[0:1, 0:1], 0.1))
        return e.reciprocal(mrow[0:1, 2:3], mrow[0:1, 1:2])
    S.op("vector", temp_chain, waits=[("ac", m_t1)], inc="dv")
    m_t3 = S.cnt["dv"]
    S.op("tensor", lambda e: e.matmul(ps_w[:, 0:1], ones1x128_v, mrow[0:1, 2:3], start=True, stop=True),
         waits=[("dv", max(m_t3, m_p2bc))], inc="pe")
    m_tb = S.cnt["pe"]
    S.op("vector", lambda e: e.tensor_copy(tinvb[:, :], ps_w[:, 0:1]), waits=[("pe", m_tb)], inc="dv")
    m_tinv = S.cnt["dv"]

    def proj(ps, w_s, xnt):
        def f(e):
            r = None
            for h in range(2):
                for c in range(8):
                    r = e.matmul(ps[h][:, :], xnt[:, c * 128:(c + 1) * 128],
                                 w_s[:, c, H(h)], start=(c == 0), stop=(c == 7))
            return r
        return f

    mk = dict(xA=[0] * NT, sq=[0] * NT, var=[0] * NT, sd=[0] * NT,
              xn=[0] * NT, xT=[0] * NT, xevac=[0] * NT, xnst=[0] * NT,
              kmm=[0] * NT, vmm=[0] * NT, kevac=[0] * NT, kv=[0] * NT,
              u=[0] * NT, ust=[0] * NT, cs=[0] * NT, sco=[0] * NT,
              d2p=[0] * NT, rB=[0] * NT, rC=[0] * NT, rsq=[0] * NT,
              rdd=[0] * NT, rsg=[0] * NT, rw1=[0] * NT, wsel=[0] * NT,
              wT=[0] * NT, wTev=[0] * NT)

    def a_load(t):
        if not (0 <= t < NT):
            return
        xw = []
        if t >= 2:
            xw += [("dv", mk["xn"][t - 2]), ("ac", mk["sq"][t - 2])]
        mk["xA"][t] = S.op("sync", (lambda e, t=t: e.dma_start(
            out=x_t[t % 2][:, :], in_=x_h[t * 128:(t + 1) * 128, :])),
            waits=xw, inc="sx%d" % (t % 2), inc_by=16)

    def a_ln(t):
        if not (0 <= t < NT):
            return
        B = 12 * (t % 2)
        R = lambda i: rows[:, B + i:B + i + 1]
        sqw = [("sx%d" % (t % 2), mk["xA"][t])]
        if t == 0:
            sqw.append(("pe", m_p2))
        mk["sq"][t] = S.op("scalar", (lambda e, t=t: e.activation(
            f4[:, :], x_t[t % 2][:, :], AF.Square, accum_out=R(1))),
            waits=sqw, inc="ac")

        def rows_chain(e, t=t, R=R):
            CH(e, e.reduce_sum(R(0), x_t[t % 2][:, :], AX.X))
            CH(e, e.scalar_tensor_tensor(R(4), R(0), 1.0 / (D * D), R(0), ALU.mult, ALU.mult))
            CH(e, e.scalar_tensor_tensor(R(5), R(1), 1.0 / D, R(4), ALU.mult, ALU.subtract))
            return e.tensor_scalar_add(R(6), R(5), EPS)
        lnw = [("ac", mk["sq"][t]), ("sx%d" % (t % 2), mk["xA"][t])]
        mk["var"][t] = S.op("vector", rows_chain, waits=lnw, inc="dv")

        def lnexp(e, R=R):
            CH(e, e.activation(R(9), R(6), AF.Ln), "chs")
            return e.activation(R(10), R(9), AF.Exp, scale=-0.5)
        m_rst = S.op("scalar", lnexp, waits=[("dv", mk["var"][t])], inc="ac")
        mk["sd"][t] = m_rst

        def rows_chain2(e, t=t, R=R):
            CH(e, e.scalar_tensor_tensor(R(11), R(0), 1.0 / D, R(10), ALU.mult, ALU.mult))
            CH(e, e.tensor_mul(R(7), R(10), R(10)))
            CH(e, e.scalar_tensor_tensor(x2all[:, t:t + 1], R(5), float(D), R(7), ALU.mult, ALU.mult))
            return e.tensor_scalar(xn_t[t % 2][:, :], x_t[t % 2][:, :],
                                   R(10), R(11), ALU.mult, ALU.subtract)
        xw = [("ac", m_rst)]
        if t >= 2:
            xw.append(("pe", mk["xT"][t - 2]))
        mk["xn"][t] = S.op("vector", rows_chain2, waits=xw, inc="dv")
        mk["nmr"] = mk.get("nmr", [0] * NT)
        mk["nmr"][t] = mk["xn"][t]

    def a_xt(t):
        if not (0 <= t < NT):
            return
        tw = [("dv", mk["xn"][t])]
        if t >= 1:
            tw.append(("ac", mk["xevac"][t - 1]))
        else:
            tw.append(("ct", m_consts))

        def xtrans(e, t=t):
            r = None
            for c in range(8):
                r = e.transpose(ps_t[:, c * 128:(c + 1) * 128], xn_t[t % 2][:, c * 128:(c + 1) * 128], id128_v)
            return r
        mk["xT"][t] = S.op("tensor", xtrans, waits=tw, inc="pe")

    def a_xevac(t):
        if not (0 <= t < NT):
            return
        ev_w = [("pe", mk["xT"][t])]
        if t >= 2:
            ev_w += [("pe", mk["sco"][t - 2]), ("xs%d" % (t % 2), mk["xnst"][t - 2])]
        mk["xevac"][t] = S.op("scalar", (lambda e, t=t: e.copy(xnt_t[t % 2][:, :], ps_t[:, :])),
                              waits=ev_w, inc="ac")
        mk["xnst"][t] = S.op("scalar", (lambda e, t=t: e.dma_start(
            out=xnt_hbm[t, :, :], in_=xnt_t[t % 2][:, :])), inc="xs%d" % (t % 2), inc_by=16)

    def a_kv(t):
        if not (0 <= t < NT):
            return
        kw = [("ac", mk["xevac"][t]), ("swk", 16)]
        kw.append(("ac", mk["kevac"][t - 1]) if t >= 1 else ("ac", m_eud))
        mk["kmm"][t] = S.op("tensor", proj(ps_a, wk_s, xnt_t[t % 2][:, :]), waits=kw, inc="pe")
        vw = [("swv", 16)]
        vw.append(("dv", mk["kv"][t - 1]) if t >= 1 else ("ac", m_dm1))
        mk["vmm"][t] = S.op("tensor", proj(ps_b, wv_s, xnt_t[t % 2][:, :]), waits=vw, inc="pe")

        def kevac(e):
            r = None
            for h in range(2):
                r = e.copy(f1[:, H(h)], ps_a[h][:, :])
            return r
        kev = [("pe", mk["kmm"][t])]
        if t >= 1:
            kev.append(("dv", mk["kv"][t - 1]))
        mk["kevac"][t] = S.op("scalar", kevac, waits=kev, inc="ac")

        def kvmul(e):
            r = None
            for h in range(2):
                r = e.tensor_mul(f2x[:, H(h)], ps_b[h][:, :], f1[:, H(h)])
            return r
        kvw = [("pe", mk["vmm"][t]), ("ac", mk["kevac"][t])]
        if t >= 1:
            kvw.append(("pl", mk["u"][t - 1]))
        mk["kv"][t] = S.op("vector", kvmul, waits=kvw, inc="dv")

    def a_u(t):
        if not (0 <= t < NT):
            return
        uw = [("dv", mk["kv"][t]), ("ac", m_dneg)]
        if t >= 2:
            uw += [("pe", mk["cs"][t - 2]), ("uss%d" % (t % 2), mk["ust"][t - 2])]
        mk["u"][t] = S.op("gpsimd", (lambda e, t=t: e.tensor_tensor(
            u_t[t % 2][:, :], f2x[:, :], dneg_s[:, :], ALU.mult)), waits=uw, inc="pl")
        mk["ust"][t] = S.op("gpsimd", (lambda e, t=t: e.dma_start(
            out=u_hbm[t, :, :], in_=u_t[t % 2][:, :])), inc="uss%d" % (t % 2), inc_by=16)

    def a_sco(t):
        if not (0 <= t < NT):
            return

        def scomm(e, t=t):
            r = None
            for c in range(8):
                r = e.matmul(ps_w[:, 0:NP], xnt_t[t % 2][:, c * 128:(c + 1) * 128],
                             pm_s[:, c, :], start=(c == 0), stop=(c == 7))
            return r
        scw = [("ac", mk["xevac"][t]), ("spm", 16)]
        scw.append(("ac", mk["d2p"][t - 1]) if t >= 1 else ("dv", m_tinv))
        mk["sco"][t] = S.op("tensor", scomm, waits=scw, inc="pe")
        mk["d2p"][t] = S.op("scalar", (lambda e, t=t: e.activation(
            smA[:, t * NP:(t + 1) * NP], ps_w[:, 0:NP], AF.Identity,
            bias=x2all[:, t:t + 1], scale=-2.0)),
            waits=[("pe", mk["sco"][t]), ("dv", mk["nmr"][t])], inc="ac")

    def a_cs(t):
        if not (0 <= t < NT):
            return

        def csmm(e, t=t):
            r = None
            for h in range(2):
                r = e.matmul(ps_cs[h][:, :], big_s[:, 256 + 16 * t:256 + 16 * (t + 1)], u_t[t % 2][:, H(h)],
                             start=(t == 0), stop=(t == NT - 1))
            return r
        csw = [("pl", mk["u"][t])]
        if t == 0:
            csw += [("ac", m_r16f2), ("ct", m_consts)]
        mk["cs"][t] = S.op("tensor", csmm, waits=csw, inc="pe")

    a_load(0)
    a_load(1)
    a_ln(0)
    for s in range(NT + 3):
        a_ln(s + 1)
        a_xt(s)
        a_xevac(s)
        a_kv(s - 1)
        a_u(s - 1)
        a_sco(s - 1)
        a_cs(s - 2)
        a_load(s + 2)

    def sh_mul(e):
        r = None
        for h in range(2):
            r = e.tensor_mul(sh_s[0:16, H(h)], ps_cs[h][0:16, :], r16_s[0:16, H(h)])
        return r
    m_sh = S.op("vector", sh_mul, waits=[("pe", mk["cs"][NT - 1]), ("ac", m_r16f2)], inc="dv")

    def spre(e):
        r = None
        for h in range(2):
            r = e.matmul(ps_a[h][0:1, :], ones128_s[0:16, 0:1], sh_s[0:16, H(h)], start=True, stop=True)
        return r
    m_spre = S.op("tensor", spre, waits=[("dv", m_sh), ("ac", mk["kevac"][NT - 1])], inc="pe")

    def srow_mul(e):
        r = None
        for h in range(2):
            r = e.tensor_mul(rowC[0:1, H(h)], ps_a[h][0:1, :], rowA[0:1, H(h)])
        return r
    m_srow = S.op("vector", srow_mul, waits=[("pe", m_spre), ("ac", m_d2047)], inc="dv")

    S.op("gpsimd", lambda e: e.dma_start(out=cc_in[:, :].opt(), in_=rowC[:, :]),
         waits=[("dv", m_srow)], inc="cw", inc_by=16)
    m_ccin = S.cnt["cw"]
    S.op("gpsimd", lambda e: e.collective_compute(
        "AllGather", ALU.bypass, replica_groups=[list(range(N_CORES))],
        ins=[cc_in.ap().opt()], outs=[cc_out.ap().opt()]),
        waits=[("cw", m_ccin), ("cc", 1)], inc="cc")
    S.op("gpsimd", lambda e: e.dma_start(out=sall_s[0:8, :], in_=cc_out[:, :].opt()),
         waits=[("cc", 2)], inc="cw", inc_by=16)
    m_sall = S.cnt["cw"]

    def gall_outer(e):
        r = None
        for h in range(2):
            r = e.matmul(ps_b[h][0:8, :], qneg_v, ed_s[0:1, H(h)], start=True, stop=True)
        return r
    m_go = S.op("tensor", gall_outer, waits=[("dv", mk["kv"][NT - 1])], inc="pe")
    m_gall = S.op("scalar", exp_evac(gall_s, ps_b, rows_=8), waits=[("pe", m_go)], inc="ac")

    def prefmm_a(e):
        r = None
        for h in range(2):
            r = e.matmul(ps_cs[h][:, :], lt16_s[:, :], sh_s[0:16, H(h)], start=True, stop=False)
        return r
    m_prefa = S.op("tensor", prefmm_a, waits=[("dv", m_sh), ("ct", m_consts)], inc="pe")

    from concourse.bass_types import AP as _AP
    G = NT
    W = G * NP
    smB, eqA = f2x, f4

    def v3(buf):
        return buf[:, 0:W].rearrange("p (g e) -> p g e", g=G)

    def gb(col):
        return _AP(col.tensor, col.offset, [col.ap[0], (col.ap[1][0], G), (0, NP)])

    def r_b1(e):
        p2c = p2bc[:, :]
        p2b3 = _AP(p2c.tensor, p2c.offset, [p2c.ap[0], (0, G), (p2c.ap[1][0], NP)])
        CH(e, e.tensor_tensor(v3(smB), v3(smA), p2b3, ALU.add))
        return e.tensor_scalar_max(smA[:, 0:W], smB[:, 0:W], 0.0)
    m_rb1 = S.op("vector", r_b1,
                 waits=[("ac", mk["d2p"][NT - 1]), ("pl", mk["u"][NT - 1]), ("dv", m_tinv)],
                 inc="dv")

    def r_b2(e):
        CH(e, e.tensor_reduce(sm1[:, 0:G], v3(smA), AX.X, ALU.min))
        CH(e, e.tensor_tensor(v3(eqA), v3(smA), gb(sm1[:, 0:G]), ALU.is_equal))
        CH(e, e.scalar_tensor_tensor(smB[:, 0:W], eqA[:, 0:W], 1e9, smA[:, 0:W], ALU.mult, ALU.add))
        CH(e, e.tensor_reduce(sm1[:, G:2 * G], v3(smB), AX.X, ALU.min))
        return e.tensor_tensor(v3(smA), v3(smB), gb(sm1[:, G:2 * G]), ALU.is_equal)
    m_rb2 = S.op("vector", r_b2, waits=[("dv", m_rb1)], inc="dv")

    def r_sq(e):
        CH(e, e.activation(sm2[:, :], sm1[:, :], AF.Ln), "chs")
        return e.activation(sm3[:, :], sm2[:, :], AF.Exp, scale=0.5)
    m_rsq = S.op("scalar", r_sq, waits=[("dv", m_rb2)], inc="ac")

    def r_dd(e):
        CH(e, e.tensor_sub(sm4[:, 0:G], sm3[:, G:2 * G], sm3[:, 0:G]))
        return e.tensor_scalar(sm4[:, G:2 * G], sm4[:, 0:G], tinvb[:, 0:1], None, ALU.mult)
    m_rdd = S.op("vector", r_dd, waits=[("ac", m_rsq)], inc="dv")
    m_rsg = S.op("scalar", lambda e: e.activation(sm5[:, 0:G], sm4[:, G:2 * G], AF.Exp, scale=-1.0),
                 waits=[("dv", m_rdd)], inc="ac")

    def r_w(e):
        CH(e, e.tensor_scalar_add(sm5[:, G:2 * G], sm5[:, 0:G], 1.0))
        CH(e, e.reciprocal(sm2[:, 0:G], sm5[:, G:2 * G]))
        CH(e, e.tensor_scalar(sm2[:, G:2 * G], sm2[:, 0:G], -1.0, 1.0, ALU.mult, ALU.add))
        CH(e, e.tensor_tensor(v3(smB), v3(smA), gb(sm2[:, 0:G]), ALU.mult))
        CH(e, e.tensor_tensor(v3(smA), v3(eqA), gb(sm2[:, G:2 * G]), ALU.mult))
        return e.tensor_add(wsall[:, :], smA[:, 0:W], smB[:, 0:W])
    m_route = S.op("vector", r_w, waits=[("ac", m_rsg)], inc="dv")
    m_prod = S.op("vector", lambda e: e.tensor_mul(prod8_s[0:8, :], gall_s[0:8, :], sall_s[0:8, :]),
                  waits=[("ac", m_gall), ("cw", m_sall)], inc="dv")

    def issue_b_tail():
        def ccol(e):
            r = None
            for h in range(2):
                r = e.matmul(ps_b[h][0:1, :], ones128_s[0:8, 0:1], prod8_s[0:8, H(h)],
                             start=True, stop=True)
            return r
        m_cc2 = S.op("tensor", ccol, waits=[("dv", m_prod)], inc="pe")

        def cr0_mul(e):
            r = None
            for h in range(2):
                r = e.tensor_mul(cr0_s[0:1, H(h)], ps_b[h][0:1, :], rowB[0:1, H(h)])
            return r
        m_cr0 = S.op("vector", cr0_mul, waits=[("pe", m_cc2), ("ac", m_d1)], inc="dv")

        def prefmm(e):
            r = None
            for h in range(2):
                r = e.matmul(ps_cs[h][:, :], ones1x16_v, cr0_s[0:1, H(h)], start=False, stop=True)
            return r
        m_pref = S.op("tensor", prefmm, waits=[("dv", m_cr0)], inc="pe")

        def cry_mul(e):
            r = None
            for h in range(2):
                r = e.tensor_mul(cr_s[0:16, H(h)], ps_cs[h][0:16, :], f2_s[0:16, H(h)])
            return r
        m_cry = S.op("vector", cry_mul, waits=[("pe", m_pref)], inc="dv")

        cs_ctx.close()
        po = [ctx.enter_context(nc.psum_tensor(f"ps_o{h}", [128, 512], F32)) for h in range(2)]
        return m_cr0, m_cry, po
    m_cr0 = m_cry = None
    ps_o = None

    c = dict(xC=[0] * NT, xnC=[0] * NT, uC=[0] * NT, wr=[0] * NT, rsig=[0] * NT,
             cum=[0] * NT, t1=[0] * NT, t2=[0] * NT, wkv=[0] * NT, rwkv=[0] * NT,
             rT=[0] * NT, rTev=[0] * NT, womm=[0] * NT, wse=[0] * NT,
             tapb=[0] * NT, tapw=[0] * NT, tapr=[0] * NT,
             hmm=[0] * NT, hev=[0] * NT, g=[0] * NT, gT=[0] * NT,
             gTev=[0] * NT, omm=[0] * NT, out=[0] * NT, yst=[0] * NT)
    out_t = [f5, f6]

    def c_load_xn(t):
        if not (0 <= t < NT) or c["xnC"][t]:
            return
        xnw = [("xs0", S.cnt["xs0"]), ("xs1", S.cnt["xs1"]), ("pe", mk["sco"][NT - 1])]
        if t >= 2:
            xnw.append(("pe", c["hmm"][t - 2]))
        c["xnC"][t] = S.op("sync", (lambda e, t=t: e.dma_start(
            out=xnt_t[t % 2][:, :], in_=xnt_hbm[t, :, :])), waits=xnw, inc="sn%d" % (t % 2), inc_by=16)

    def c_load(t):
        if not (0 <= t < NT) or c["uC"][t]:
            return
        c_load_xn(t)
        uw = [("uss0", S.cnt["uss0"]), ("uss1", S.cnt["uss1"]), ("pe", mk["cs"][NT - 1])]
        if t >= 2:
            uw += [("pe", c["cum"][t - 2]), ("pl", c["t2"][t - 2])]
        c["uC"][t] = S.op("sync", (lambda e, t=t: e.dma_start(
            out=u_t[t % 2][:, :], in_=u_hbm[t, :, :])), waits=uw, inc="su%d" % (t % 2), inc_by=16)
        xw = [("dv", mk["xn"][NT - 1]), ("ac", mk["sq"][NT - 1])]
        if t >= 2:
            xw.append(("dv", c["out"][t - 2]))
        c["xC"][t] = S.op("sync", (lambda e, t=t: e.dma_start(
            out=x_t[t % 2][:, :], in_=x_h[t * 128:(t + 1) * 128, :])),
            waits=xw, inc="sx%d" % (t % 2), inc_by=16)

    def c_rT(t):
        if not (0 <= t < NT):
            return
        rtw = [("dv", c["rwkv"][t])]
        if t == 0:
            rtw += [("ac", c["gTev"][2]), ("ac", c["wTev"][2])]
        elif t == 1:
            rtw += [("ac", c["rTev"][0]), ("ac", c["wTev"][1])]
        elif t == 2:
            rtw += [("ac", c["rTev"][1]), ("ac", c["gTev"][2]), ("ac", c["wTev"][2])]
        elif t == 3:
            rtw += [("ac", c["rTev"][2]), ("ac", c["wTev"][3])]
        else:
            rtw += [("ac", c["gTev"][t - 1]), ("ac", c["wTev"][t])]

        def rtrans(e, t=t):
            r = None
            for cc_ in range(8):
                r = e.transpose(ps_t[:, cc_ * 128:(cc_ + 1) * 128],
                                rwkv_t[t % 2][:, cc_ * 128:(cc_ + 1) * 128], id128_v)
            return r
        c["rT"][t] = S.op("tensor", rtrans, waits=rtw, inc="pe")
        rtev = [("pe", c["rT"][t])]
        if t >= 2:
            rtev.append(("pe", c["womm"][t - 2]))
        c["rTev"][t] = S.op("scalar", (lambda e, t=t: e.copy(rwkvT_t[t % 2][:, :], ps_t[:, :])),
                            waits=rtev, inc="ac")

    def c_h(t):
        if not (0 <= t < NT) or c["hmm"][t]:
            return

        def hmm(e, t=t):
            r = None
            for cc_ in range(8):
                r = e.matmul(ps_a[0][:, :], xnt_t[t % 2][:, cc_ * 128:(cc_ + 1) * 128],
                             b_s[:, cc_, :], start=(cc_ == 0), stop=(cc_ == 7))
            return r
        c["hmm"][t] = S.op("tensor", hmm, waits=[("ac", c["rsig"][t])], inc="pe")
        hev = [("pe", c["hmm"][t])]
        if t >= 1:
            hev.append(("dv", c["g"][t - 1]))
        c["hev"][t] = S.op("scalar", lambda e: e.copy(h_s[:, :], ps_a[0][:, :]),
                           waits=hev, inc="ac")

    def c_g(t):
        if not (0 <= t < NT) or c["g"][t]:
            return
        gw = [("ac", c["hev"][t])]
        if t >= 1:
            gw.append(("pe", c["gT"][t - 1]))

        def gmul(e, t=t):
            wv = wsall[:, t * NP:(t + 1) * NP]
            from concourse.bass_types import AP as _AP
            bc = _AP(wv.tensor, wv.offset, [wv.ap[0], (1, NP), (0, RK)])
            return e.tensor_tensor(g_bf[:, :].rearrange("p (e r) -> p e r", e=NP),
                                   h_s[:, :].rearrange("p (e r) -> p e r", e=NP),
                                   bc, ALU.mult)
        c["g"][t] = S.op("vector", gmul, waits=gw, inc="dv")

    def c_wsT(t):
        if not (0 <= t < NT) or c["wse"][t]:
            return
        if t == 0:
            ww = [("dv", m_route), ("ac", mk["xevac"][NT - 1])]
        elif t in (1, 2):
            ww = [("dv", m_route), ("ac", c["wTev"][t - 1])]
        else:
            ww = [("dv", m_route), ("ac", c["rTev"][t - 1])]
        c["wse"][t] = S.op("tensor", (lambda e, t=t: e.transpose(
            ps_t[0:NP, 512:640], wsall[:, t * NP:(t + 1) * NP], id128_v)),
            waits=ww, inc="pe")
        wtw = [("pe", c["wse"][t])]
        if t >= 3:
            wtw.append(("pe", c["omm"][t - 3]))
        c["wTev"] = c.get("wTev", [0] * NT)
        c["wTev"][t] = S.op("scalar", (lambda e, t=t: e.copy(
            wselT_t[t % 3][:, :], ps_t[0:NP, 512:640])), waits=wtw, inc="ac")

    def c_wr(t):
        if not (0 <= t < NT) or c["wr"][t]:
            return
        wrw = [("sn%d" % (t % 2), c["xnC"][t]), ("wt", m_wall)]
        if t >= 1:
            wrw += [("ac", c["hev"][t - 1]), ("ac", c["rsig"][t - 1])]
        else:
            wrw.append(("dv", m_srow))
        c["wr"][t] = S.op("tensor", proj(ps_a, wr_s, xnt_t[t % 2][:, :]), waits=wrw, inc="pe")

        def rsig(e, t=t):
            r = None
            for h in range(2):
                r = e.activation(rb_t[t % 3][:, H(h)], ps_a[h][:, :], AF.Sigmoid)
            return r
        rw = [("pe", c["wr"][t])]
        if t >= 3:
            rw.append(("dv", c["rwkv"][t - 3]))
            if _DEBUG_RWKV:
                rw.append(("dbg", c["tapb"][t - 3]))
        c["rsig"][t] = S.op("scalar", rsig, waits=rw, inc="ac")

    def c_scan(t):
        if not (0 <= t < NT):
            return

        def cummm(e, t=t):
            r = None
            for h in range(2):
                e.matmul(ps_b[h][:, :], lt128_v, u_t[t % 2][:, H(h)], start=True, stop=False)
                r = e.matmul(ps_b[h][:, :], ohp_s[0:16, 128 * t:128 * (t + 1)], cr_s[0:16, H(h)], start=False, stop=True)
            return r
        cw = [("su%d" % (t % 2), c["uC"][t]), ("dv", m_cry)]
        cw.append(("dv", c["t1"][t - 1]) if t >= 1 else ("ac", m_gall))
        if t == 0:
            cw.append(("ct", m_consts))
        c["cum"][t] = S.op("tensor", cummm, waits=cw, inc="pe")

        def t1mul(e):
            r = None
            for h in range(2):
                r = e.tensor_mul(f2x[:, H(h)], ps_b[h][:, :], dm1_s[:, H(h)])
            return r
        c["t1"][t] = S.op("vector", t1mul, waits=[("pe", c["cum"][t]), ("ac", m_dm1)], inc="dv")

        t2w = [("su%d" % (t % 2), c["uC"][t]), ("ac", m_eud)]
        if t >= 1:
            t2w.append(("dv", c["wkv"][t - 1]))
        c["t2"][t] = S.op("gpsimd", (lambda e, t=t: e.tensor_tensor(
            f3[:, :], u_t[t % 2][:, :], eud_s[:, :], ALU.mult)), waits=t2w, inc="pl")

        wkvw = [("pl", c["t2"][t]), ("dv", c["t1"][t])]
        if _DEBUG_RWKV and t >= 1:
            wkvw.append(("dbg", c["tapw"][t - 1]))
        c["wkv"][t] = S.op("vector", lambda e: e.tensor_add(f4[:, :], f2x[:, :], f3[:, :]),
                           waits=wkvw, inc="dv")
        rkw = [("ac", c["rsig"][t]), ("dv", c["wkv"][t])]
        if t >= 1:
            rkw.append(("pe", c["rT"][t - 1]))
        if _DEBUG_RWKV and t >= 2:
            rkw.append(("dbg", c["tapr"][t - 2]))
        c["rwkv"][t] = S.op("vector", (lambda e, t=t: e.tensor_mul(
            rwkv_t[t % 2][:, :], f4[:, :], rb_t[t % 3][:, :])), waits=rkw, inc="dv")
        if _DEBUG_RWKV:
            c["tapw"][t] = S.op("gpsimd", (lambda e, t=t: e.dma_start(
                out=dbg_wkv[t, :, :], in_=f4[:, :])),
                waits=[("dv", c["wkv"][t])], inc="dbg", inc_by=16)
            c["tapb"][t] = S.op("gpsimd", (lambda e, t=t: e.dma_start(
                out=dbg_rb[t, :, :], in_=rb_t[t % 3][:, :])),
                waits=[("ac", c["rsig"][t])], inc="dbg", inc_by=16)
            c["tapr"][t] = S.op("gpsimd", (lambda e, t=t: e.dma_start(
                out=dbg_rwkv[t, :, :], in_=rwkv_t[t % 2][:, :])),
                waits=[("dv", c["rwkv"][t])], inc="dbg", inc_by=16)

    def c_womm(t):
        if not (0 <= t < NT):
            return

        def womm(e, t=t):
            r = None
            for h in range(2):
                for cc_ in range(8):
                    r = e.matmul(ps_o[h][:, :], rwkvT_t[t % 2][:, cc_ * 128:(cc_ + 1) * 128],
                                 wo_s[:, cc_, H(h)], start=(cc_ == 0), stop=False)
            return r
        wow = [("ac", c["rTev"][t])]
        if t >= 1:
            wow.append(("dv", c["out"][t - 1]))
        c["womm"][t] = S.op("tensor", womm, waits=wow, inc="pe")

    def c_gT(t):
        if not (0 <= t < NT) or c["gT"][t]:
            return

        def gtrans(e):
            r = None
            for m in range(4):
                r = e.transpose(ps_t[:, m * 128:(m + 1) * 128], g_bf[:, m * 128:(m + 1) * 128], id128_v)
            return r
        if t == 0:
            gw2 = [("dv", c["g"][t]), ("ac", mk["xevac"][NT - 1])]
        elif t in (1, 2):
            gw2 = [("dv", c["g"][t]), ("ac", c["gTev"][t - 1])]
        else:
            gw2 = [("dv", c["g"][t]), ("ac", c["rTev"][t])]
        c["gT"][t] = S.op("tensor", gtrans, waits=gw2, inc="pe")
        gtev = [("pe", c["gT"][t])]
        if t >= 3:
            gtev.append(("pe", c["omm"][t - 3]))
        c["gTev"][t] = S.op("scalar", (lambda e, t=t: e.copy(gT_t[t % 3][:, :], ps_t[:, 0:ER])),
                            waits=gtev, inc="ac")

    def c_am(t):
        if not (0 <= t < NT):
            return

        def ammm(e, t=t):
            r = None
            for h in range(2):
                for m in range(4):
                    e.matmul(ps_o[h][:, :], gT_t[t % 3][:, m * 128:(m + 1) * 128],
                             a_s[:, m, H(h)], start=False, stop=False)
                r = e.matmul(ps_o[h][:, :], wselT_t[t % 3][:, :], bias_s[:, H(h)],
                             start=False, stop=True)
            return r
        omw = [("ac", c["gTev"][t]), ("ac", c["wTev"][t])]
        if t < 2:
            omw.append(("dv", m_cry))
        c["omm"][t] = S.op("tensor", ammm, waits=omw, inc="pe")

        def outadd(e, t=t):
            r = None
            for h in range(2):
                r = e.tensor_add(out_t[t % 2][:, H(h)], ps_o[h][:, :], x_t[t % 2][:, H(h)])
            return r
        ow = [("pe", c["omm"][t]), ("sx%d" % (t % 2), c["xC"][t])]
        if t >= 2:
            ow.append(("ys%d" % (t % 2), c["yst"][t - 2]))
        c["out"][t] = S.op("vector", outadd, waits=ow, inc="dv")
        c["yst"][t] = S.op("scalar", (lambda e, t=t: e.dma_start(
            out=y_h[t * 128:(t + 1) * 128, :], in_=out_t[t % 2][:, :])),
            waits=[("dv", c["out"][t])], inc="ys%d" % (t % 2), inc_by=16)

    c_load(0)
    c_load(1)
    c_wr(0)
    c_h(0)
    c_wsT(0)
    c_g(0)
    c_wr(1)
    c_gT(0)
    c_h(1)
    c_wsT(1)
    c_g(1)
    c_gT(1)
    c_load_xn(2)
    c_wr(2)
    c_h(2)
    c_wsT(2)
    c_g(2)
    c_gT(2)
    c_load_xn(3)

    m_cr0, m_cry, ps_o = issue_b_tail()

    for s in range(NT + 2):
        c_wsT(s - 1)
        c_h(s - 1)
        c_rT(s - 1)
        c_g(s - 1)
        c_wr(s)
        c_gT(s - 1)
        c_scan(s)
        c_womm(s - 1)
        c_am(s - 1)
        c_load(s + 1)

    if _DEBUG_TAPS:
        taps_sb = {
            "ed": ed_s, "sh": sh_s, "cr": cr_s, "cr0": cr0_s, "x2": x2all,
            "rows": rows, "f1": f1, "f2x": f2x, "f3": f3, "f4": f4,
            "p2bc": p2bc, "tinvb": tinvb, "sm1": sm1, "sm3": sm3, "sm4": sm4,
            "dneg": dneg_s, "dm1": dm1_s, "eud": eud_s, "r16": r16_s, "f2c": f2_s,
            "hs": h_s, "gbf": g_bf, "rowC": rowC, "mrow": mrow, "wtall": wtall,
        }
        taps_dram = {"u": (u_hbm, [NT, 128, D], BF16), "xnt": (xnt_hbm, [NT, 128, D], BF16)}
        endw = [("ys0", S.cnt["ys0"]), ("ys1", S.cnt["ys1"]), ("dv", S.cnt["dv"]),
                ("ac", S.cnt["ac"]), ("pe", S.cnt["pe"]), ("pl", S.cnt["pl"])]
        for name in _DEBUG_TAPS:
            if name in taps_sb:
                t = taps_sb[name]
                o = nc.dram_tensor("dbg_" + name, list(t.shape), t.dtype, kind="ExternalOutput").ap()
                S.op("sync", (lambda e, t=t, o=o: e.dma_start(
                    out=o, in_=t[tuple(slice(None) for _ in t.shape)])), waits=endw, inc="dbg", inc_by=16)
            else:
                src_, shp, dt = taps_dram[name]
                o = nc.dram_tensor("dbg_" + name, shp, dt, kind="ExternalOutput").ap()
                S.op("sync", (lambda e, s=src_, o=o: e.dma_start(out=o, in_=s)),
                     waits=endw, inc="dbg", inc_by=16)
        S.op("sync", lambda e: e.nop(), waits=[("dbg", S.cnt["dbg"])])

    S.op("scalar", lambda e: e.nop(), waits=[("ys0", S.cnt["ys0"]), ("ys1", S.cnt["ys1"])])

    with nc.Block() as block:
        @block.sync
        def _(e):
            S.run(e, "sync", sems)

        @block.scalar
        def _(e):
            S.run(e, "scalar", sems)

        @block.vector
        def _(e):
            S.run(e, "vector", sems)

        @block.tensor
        def _(e):
            S.run(e, "tensor", sems)

        @block.gpsimd
        def _(e):
            S.run(e, "gpsimd", sems)

    sem_ctx.close()
    ctx.close()
    return nc


def _consts():
    i_ = np.arange(128, dtype=np.float32)
    lt128 = (i_[:, None] < i_[None, :]).astype(ml_dtypes.bfloat16)
    id128 = np.eye(128, dtype=ml_dtypes.bfloat16)
    g16 = np.arange(16, dtype=np.float32)
    lt16 = (g16[:, None] < g16[None, :]).astype(np.float32)
    eg = np.zeros((128, NT, 16), dtype=ml_dtypes.bfloat16)
    for g in range(NT):
        eg[:, g, g] = 1.0
    oh16 = np.zeros((16, NT, 128), dtype=ml_dtypes.bfloat16)
    for g in range(NT):
        oh16[g, g, :] = 1.0
    emat = np.zeros((NP, ER), dtype=ml_dtypes.bfloat16)
    for e in range(NP):
        emat[e, e * RK:(e + 1) * RK] = 1.0
    bigc = np.concatenate([lt128, id128, eg.reshape(128, NT * 16)], axis=1)
    crow_tail = np.zeros((1, 551), np.float32)
    crow_tail[0, 7:135] = i_
    crow_tail[0, 135:263] = -i_
    crow_tail[0, 263:391] = i_ - 1.0
    crow_tail[0, 391:407] = g16 * 128.0
    crow_tail[0, 407:423] = 1.0
    crow_tail[0, 423:551] = 1.0
    return {
        "bigc": np.ascontiguousarray(bigc),
        "ohp": np.ascontiguousarray(oh16.reshape(16, NT * 128)),
        "lt16": lt16,
        "ones128": np.ones((128, 1), np.float32),
        "emat": emat,
        "_crow_tail": crow_tail,
    }


def kernel(L0_attn_x, L0_attn_Wk, L0_attn_Wv, L0_attn_Wr, L0_attn_Wo,
           L0_attn_time_decay, L0_attn_time_first,
           L0_ffn_prototypes, L0_ffn_B, L0_ffn_A, L0_ffn_bias, L0_ffn_temp):
    x = np.ascontiguousarray(np.asarray(L0_attn_x, np.float32).reshape(SEQ, D))
    shard = [np.ascontiguousarray(x[cix * TPC:(cix + 1) * TPC]) for cix in range(N_CORES)]
    common = {
        "wk": np.ascontiguousarray(np.asarray(L0_attn_Wk, np.float32).T),
        "wv": np.ascontiguousarray(np.asarray(L0_attn_Wv, np.float32).T),
        "wr": np.ascontiguousarray(np.asarray(L0_attn_Wr, np.float32).T),
        "wo": np.ascontiguousarray(np.asarray(L0_attn_Wo, np.float32).T),
        "bm": np.ascontiguousarray(np.asarray(L0_ffn_B, np.float32).transpose(2, 0, 1).reshape(D, ER)),
        "am": np.ascontiguousarray(np.asarray(L0_ffn_A, np.float32).transpose(0, 2, 1).reshape(ER, D)),
        "pm": np.ascontiguousarray(np.asarray(L0_ffn_prototypes, np.float32).T),
        "bias": np.ascontiguousarray(np.asarray(L0_ffn_bias, np.float32)),
    }
    common.update(_consts())
    crow_tail = common.pop("_crow_tail")
    crow_base = np.zeros((1, 2608), np.float32)
    crow_base[0, 0:1024] = np.asarray(L0_attn_time_decay, np.float32).reshape(-1)
    crow_base[0, 1024:2048] = np.asarray(L0_attn_time_first, np.float32).reshape(-1)
    crow_base[0, 2048] = np.asarray(L0_ffn_temp, np.float32).reshape(-1)[0]
    crow_base[0, 2057:2608] = crow_tail[0, :]
    in_maps = []
    for cix in range(N_CORES):
        qneg = np.full(N_CORES, -1e9, np.float32)
        for k in range(cix):
            qneg[k] = -2048.0 * (cix - 1 - k)
        crow = crow_base.copy()
        crow[0, 2049:2049 + N_CORES] = qneg
        m = dict(common)
        m["x"] = shard[cix]
        m["crow"] = crow
        in_maps.append(m)
    nc = build_program()
    res = run_bass_kernel_spmd(nc, in_maps, core_ids=list(range(N_CORES)))
    y = np.concatenate([res.results[cix]["y"] for cix in range(N_CORES)], axis=0)
    return y.reshape(1, SEQ, D)
